# version 29
# baseline (speedup 1.0000x reference)
import numpy as np
import ml_dtypes
from collections import defaultdict
from contextlib import ExitStack

import concourse.bass as bass
import concourse.mybir as mybir
from concourse.bass_utils import run_bass_kernel_spmd

F32 = mybir.dt.float32
BF16 = mybir.dt.bfloat16
AF = mybir.ActivationFunctionType
ALU = mybir.AluOpType
BFnp = ml_dtypes.bfloat16

S = 2048
D = 1024
H = 16
FF = 4096
NSUB = S // 128
NGRP = 8
EPS_LN = 1e-5
EPS_ATTN = 1e-6

ENGS = ("pe", "act", "dve", "sync", "gp")
ESEM = {"pe": "pe", "act": "act", "dve": "dve", "sync": "sm", "gp": "gb0"}
COMPUTE_SEMS = ("pe", "act", "dve", "gpc")


class Plan:

    def __init__(self):
        self.items = {e: [] for e in ENGS}
        self.cnt = defaultdict(int)
        self.lastw = {e: {} for e in ENGS}
        self.reg = {}
        self.nstages = 0
        self.limit = None

    def _emit(self, eng, fn, waits, inc, inc_sem):
        merged = {}
        for t in waits:
            if t is None:
                continue
            s, v = t
            if v > merged.get(s, -1):
                merged[s] = v
        wl = []
        for s, v in merged.items():
            if v > self.lastw[eng].get(s, -1):
                wl.append((s, v))
                self.lastw[eng][s] = v
        tok = None
        rec = None
        if inc:
            sem = inc_sem or ESEM[eng]
            amt = 1 if sem in COMPUTE_SEMS else 16
            self.cnt[sem] += amt
            tok = (sem, self.cnt[sem])
            rec = (sem, amt)
        self.items[eng].append((wl, fn, rec))
        return tok

    def stage(self, eng, fns, reads=(), writes=(), extra=(), inc_sem=None):
        self.nstages += 1
        if self.limit is not None and self.nstages > self.limit:
            return ("sm", 0)
        waits = list(extra)
        for r in reads:
            rec = self.reg.get(r)
            if rec and rec["w"] is not None:
                waits.append(rec["w"])
        for wr in writes:
            rec = self.reg.get(wr)
            if rec:
                if rec["w"] is not None:
                    waits.append(rec["w"])
                waits.extend(rec["r"])
        tok = None
        n = len(fns)
        for k, fn in enumerate(fns):
            tok = self._emit(eng, fn, waits if k == 0 else (),
                             inc=(k == n - 1), inc_sem=inc_sem)
        for r in reads:
            self.reg.setdefault(r, {"w": None, "r": []})["r"].append(tok)
        for wr in writes:
            self.reg[wr] = {"w": tok, "r": []}
        return tok

    def set_writer(self, region, tok):
        self.reg[region] = {"w": tok, "r": []}

    def raw_wait(self, eng, tok):
        self.items[eng].append(([tok], None, None))


def build_program(trunc=None, stage_limit=None):

    nc = bass.Bass("TRN2", target_bir_lowering=False, debug=False)

    d_x = nc.dram_tensor("x", [S, D], F32, kind="ExternalInput")
    d_wqkv = nc.dram_tensor("wqkv", [D, 3 * D], F32, kind="ExternalInput")
    d_bkv = nc.dram_tensor("bkv", [1, 2 * D], BF16, kind="ExternalInput")
    d_bqc = nc.dram_tensor("bqc", [128, 8], F32, kind="ExternalInput")
    d_w1 = nc.dram_tensor("w1", [D, FF], F32, kind="ExternalInput")
    d_b1c = nc.dram_tensor("b1c", [128, 32], F32, kind="ExternalInput")
    d_w2 = nc.dram_tensor("w2", [FF, D], F32, kind="ExternalInput")
    d_b2 = nc.dram_tensor("b2row", [1, D], BF16, kind="ExternalInput")
    d_grep = nc.dram_tensor("grep", [128, D], F32, kind="ExternalInput")
    d_brep = nc.dram_tensor("brep", [128, D], F32, kind="ExternalInput")
    d_id = nc.dram_tensor("ident", [128, 128], BF16, kind="ExternalInput")
    d_const = nc.dram_tensor("consts", [128, 4], F32, kind="ExternalInput")
    d_out = nc.dram_tensor("out", [S, D], F32, kind="ExternalOutput")

    p = Plan()
    p.limit = stage_limit
    stack = ExitStack()
    sb = lambda n, shp, dt: stack.enter_context(nc.sbuf_tensor(n, shp, dt))
    psa = lambda n, shp, dt: stack.enter_context(nc.psum_tensor(n, shp, dt))

    R = sb("R", [128, 32768], BF16)
    WQOFF = 8192
    W1 = sb("W1", [128, 8 * FF], BF16)
    PQT = sb("PQT", [128, 8 * S], BF16)
    H1T = sb("H1T", [128, 8 * 256], BF16)
    XT = sb("XT", [128, 4 * D], F32)
    PK = sb("PK", [128, D], BF16)
    V65 = sb("V65", [128, H * 65], BF16)
    ATTN = sb("ATTN", [128, D], F32)
    H1B = ATTN[:, 0:512].bitcast(BF16)
    STG2 = sb("STG2", [128, D], F32)
    T1 = sb("T1", [128, D], F32)
    WSTG = sb("WSTG", [128, D], F32)
    KVB = sb("KVB", [128, 8 * 130], BF16)
    KVSB = sb("KVSB", [128, 8 * 130], F32)
    GREP = KVSB[:, 0:1024]
    BQC = sb("BQC", [128, 8], F32)
    B1C = sb("B1C", [128, 32], F32)
    IDENT = sb("IDENT", [128, 128], BF16)
    ONES = sb("ONES", [1, 128], BF16)
    SCR = sb("SCR", [128, 14 * 12], F32)
    CONSTS = sb("CONSTS", [128, 4], F32)
    DEN = sb("DEN", [128, H], F32)
    UTC = STG2[:, 0:256].bitcast(BF16)
    BR1 = STG2[:, 256:768]
    BR2 = STG2[:, 768:1024]
    BR3 = V65[:, 0:1024].bitcast(F32)[:, 0:256]
    B2ROW = PK[:, :]
    BKV = R[0:1, 6144:8192]

    nc.const_aps.register(F32, EPS_LN, CONSTS[:, 0:1])
    nc.const_aps.register(F32, 0.0, CONSTS[:, 1:2])
    nc.const_aps.register(F32, 1.0, CONSTS[:, 2:3])
    nc.const_aps.register(F32, EPS_ATTN, CONSTS[:, 3:4])

    PB = [psa(f"pb{i}", [128, 512], F32) for i in range(6)]
    TRAB = [psa("tra", [128, 512], F32), psa("trb", [128, 512], F32)]
    TRBF = [t[:, :].bitcast(BF16) for t in TRAB]
    TRREG = ["tra", "trb"]
    A_PROJ = [0, 1, 5, 2]
    KVBANKS = [2, 3, 4]
    EOB = [[0, 1], [2, 3]]

    def kv_pair_ap(pr):
        return PB[KVBANKS[pr // 3]][:, (pr % 3) * 130:((pr % 3) + 1) * 130]

    def kv_bank(pr):
        return f"pb{KVBANKS[pr // 3]}"

    ln_id = [0]

    def ln_stats(src_ap, src_regions):
        slot = ln_id[0] % 12
        ln_id[0] += 1
        off = slot * 14
        st = SCR[:, off:off + 12]
        mv = SCR[:, off + 12:off + 14]
        rstd = mv[:, 1:2]
        rg = f"scr{slot}"
        p.stage("dve", [lambda e, a=st, b=src_ap: e.bn_stats(a[:, 0:6], b[:, 0:512]),
                        lambda e, a=st, b=src_ap: e.bn_stats(a[:, 6:12], b[:, 512:1024])],
                reads=list(src_regions), writes=[rg])
        p.stage("dve", [lambda e, a=mv, b=st: e.bn_aggr(a, b)], reads=[rg], writes=[rg])
        p.stage("act", [lambda e, a=rstd, b=mv: e.activation(a, b[:, 1:2], AF.Sqrt,
                                                           bias=EPS_LN, scale=1.0)],
                reads=[rg, "consts"], writes=[rg])
        p.stage("dve", [lambda e, a=rstd: e.reciprocal(a, a)], reads=[rg], writes=[rg])
        return mv[:, 0:1], rstd, rg

    class Streamer:

        def __init__(self, qe, bufs, buf_regions, buf_sems, cast_engs):
            self.qe = qe
            self.bufs = bufs
            self.bregs = buf_regions
            self.bsems = buf_sems
            self.cast_engs = cast_engs
            self.pending = []
            self.ci = 0
            self.bi = 0

        def dma(self, dram_ap, dest_ap, dest_regions, extra=()):
            b = self.bi % len(self.bufs)
            self.bi += 1
            while any(pb == b for (pb, _, _) in self.pending):
                self.flush(1)
            p.stage(self.qe, [lambda e, o=self.bufs[b], i=dram_ap: e.dma_start(o, i)],
                    writes=[self.bregs[b]], extra=list(extra), inc_sem=self.bsems[b])
            self.pending.append((b, dest_ap, dest_regions))

        def flush(self, n=10 ** 9):
            done = 0
            while self.pending and done < n:
                b, dest, dregs = self.pending.pop(0)
                eng = self.cast_engs[self.ci % len(self.cast_engs)]
                self.ci += 1
                if eng == "act":
                    fn = lambda e, o=dest, i=self.bufs[b]: e.activation(o, i, AF.Copy)
                else:
                    fn = lambda e, o=dest, i=self.bufs[b]: e.tensor_copy(o, i)
                p.stage(eng, [fn], reads=[self.bregs[b]], writes=list(dregs))
                done += 1

    wstg_bufs = [WSTG[:, i * 512:(i + 1) * 512] for i in range(2)]
    stg2_bufs = [STG2[:, i * 512:(i + 1) * 512] for i in range(2)]
    gp_st = Streamer("gp", wstg_bufs, ["wstg0", "wstg1"], ["gb0", "gb1"], ["act", "dve"])
    sc_st = Streamer("sync", stg2_bufs, ["stg2b0", "stg2b1"], ["sb0", "sb1"], ["dve", "act"])

    def wq_piece(k, c2):
        return (d_wqkv[k * 128:(k + 1) * 128, c2 * 512:(c2 + 1) * 512],
                R[:, WQOFF + k * 3072 + c2 * 512: WQOFF + k * 3072 + (c2 + 1) * 512],
                [f"wq{k}.{c2}"])

    def w1_piece(k, j):
        return (d_w1[k * 128:(k + 1) * 128, j * 512:(j + 1) * 512],
                W1[:, k * FF + j * 512: k * FF + (j + 1) * 512], [f"w1p{k}.{j}"])

    def w2_piece(f, nn):
        regs = [f"w2p{f}.{nn}"]
        addr = f * 1024 + nn * 512 - WQOFF
        if addr >= 0:
            k = addr // 3072
            c2 = (addr % 3072) // 512
            regs.append(f"wq{k}.{c2}")
        return (d_w2[f * 128:(f + 1) * 128, nn * 512:(nn + 1) * 512],
                R[:, f * 1024 + nn * 512: f * 1024 + (nn + 1) * 512], regs)

    smalls = [
        (IDENT[:, :], d_id[:, :], "ident"),
        (CONSTS[:, :], d_const[:, :], "consts"),
        (BKV, d_bkv[:, :], "bkv"),
        (BQC[:, :], d_bqc[:, :], "bqc"),
        (B1C[:, :], d_b1c[:, :], "b1c"),
    ]
    for (dst, srcd, rg) in smalls:
        p.stage("sync", [lambda e, o=dst, i=srcd: e.dma_start(o, i)],
                writes=[rg], inc_sem="sm")
    sm_all = ("sm", p.cnt["sm"])
    for (_, _, rg) in smalls:
        p.set_writer(rg, sm_all)
    bkv_regions = [f"w2p{fk}.{nn}" for fk in (6, 7) for nn in (0, 1)]
    for rg in bkv_regions:
        p.set_writer(rg, sm_all)

    p.stage("dve", [lambda e: e.memset(ONES[:, :], 1.0)], writes=["ones"])
    p.stage("dve", [lambda e: e.memset(V65[:, :], 1.0)], writes=["v65"])
    p.stage("dve", [lambda e: e.memset(KVB[:, :], 0.0)], writes=["kvb"])

    tx_reg = ["xt0", "xt1", "xt2", "xt3"]
    for sl in range(4):
        p.stage("sync", [lambda e, s=sl: e.dma_start(
            XT[:, s * D:(s + 1) * D], d_x[s * 128:(s + 1) * 128, :])],
                writes=[tx_reg[sl]], inc_sem=f"x{sl}")

    kv_pieces = [(k, c2) for c2 in (2, 3, 4, 5) for k in range(8)]
    q_pieces = [(k, c2) for c2 in (0, 1) for k in range(8)]
    w1_pieces = [(k, j) for j in range(8) for k in range(8)]
    w2e_pieces = [(f, n) for f in range(6) for n in range(2)]
    w2r_pieces = [(f, n) for f in range(6, 32) for n in range(2)]

    nbatches = 4 if trunc is None or trunc[0] != "P" else trunc[1]
    for r in range(nbatches):
        for t in range(4):
            gp_st.dma(*wq_piece(*kv_pieces[r * 8 + t]))
            sc_st.dma(*wq_piece(*kv_pieces[r * 8 + 4 + t]))
        gp_st.flush()
        sc_st.flush()

    proj_i = [0]
    a_subs = NSUB if trunc is None or trunc[0] != "A" else trunc[1]
    bc_grps = NGRP if trunc is None or trunc[0] != "BC" else trunc[1]
    if trunc is not None and trunc[0] in ("A", "P"):
        bc_grps = -1
        if trunc[0] == "P":
            a_subs = 0
    for i in range(a_subs):
        half = i % 2
        slot = i % 4
        xa = XT[:, slot * D:(slot + 1) * D]
        xreg = tx_reg[slot]

        gp_st.flush()
        sc_st.flush()
        if i < 2:
            for t in range(4):
                gp_st.dma(*wq_piece(*q_pieces[i * 8 + t]))
                sc_st.dma(*wq_piece(*q_pieces[i * 8 + 4 + t]))
        elif i < 13:
            for t in range(3):
                idx = (i - 2) * 6 + t * 2
                if idx < 64:
                    gp_st.dma(*w1_piece(*w1_pieces[idx]))
                if idx + 1 < 64:
                    sc_st.dma(*w1_piece(*w1_pieces[idx + 1]))
        if 8 <= i < 14:
            gp_st.dma(*w2_piece(*w2e_pieces[(i - 8) * 2]))
            sc_st.dma(*w2_piece(*w2e_pieces[(i - 8) * 2 + 1]))

        mean, rstd, rg = ln_stats(xa, [xreg])
        p.stage("dve", [lambda e, o=H1B[:, :], a=xa, r=rstd, m=mean:
                        e.tensor_scalar(o, a, m, r, ALU.subtract, ALU.mult)],
                reads=[xreg, rg], writes=["h1b", "attn.0"])

        for hh in range(2):
            fns = []
            for b in range(4):
                c = hh * 4 + b
                fns.append(lambda e, o=TRBF[hh][:, b * 128:(b + 1) * 128],
                           ii=H1B[:, c * 128:(c + 1) * 128]: e.transpose(o, ii, IDENT[:, :]))
            p.stage("pe", fns, reads=["h1b", "ident"], writes=[TRREG[hh]])
            out3d = H1T[:, :].rearrange("p (c t) -> p c t", t=256)[:, hh * 4:hh * 4 + 4,
                                                                  half * 128:(half + 1) * 128]
            in3d = TRBF[hh][:, 0:512].rearrange("p (c t) -> p c t", t=128)
            p.stage("act", [lambda e, o=out3d, ii=in3d: e.copy(o, ii)],
                    reads=[TRREG[hh]], writes=["h1t"])

        for j in range(4):
            bidx = A_PROJ[proj_i[0] % 4]
            proj_i[0] += 1
            bank = PB[bidx]
            c2 = 2 + j
            fns = []
            for k in range(8):
                lhs = H1T[:, k * 256 + half * 128: k * 256 + (half + 1) * 128]
                rhs = R[:, WQOFF + k * 3072 + c2 * 512: WQOFF + k * 3072 + (c2 + 1) * 512]
                fns.append(lambda e, o=bank[:, :], l=lhs, r=rhs, st=(k == 0):
                           e.matmul(o, l, r, start=st, stop=False))
            fns.append(lambda e, o=bank[:, :], r=BKV[0:1, j * 512:(j + 1) * 512]:
                       e.matmul(o, ONES[0:1, 0:128], r, start=False, stop=True))
            p.stage("pe", fns,
                    reads=["h1t", "ones"] + bkv_regions + [f"wq{k}.{c2}" for k in range(8)],
                    writes=[f"pb{bidx}"])
            if j < 2:
                tb = T1[:, (j % 2) * 512:(j % 2) * 512 + 512]
                treg = f"t1.{j % 2}"
                tw = [treg]
                p.stage("dve", [lambda e, o=tb, ii=bank[:, :]:
                                e.tensor_single_scalar(o, ii, 0.0, ALU.min)],
                        reads=[f"pb{bidx}"], writes=tw)
                p.stage("act", [lambda e, o=tb, ii=tb: e.activation(o, ii, AF.Exp)],
                        reads=[treg], writes=tw)
                p.stage("dve", [lambda e, o=PK[:, j * 512:(j + 1) * 512], ii=bank[:, :], t=tb:
                                e.scalar_tensor_tensor(o, ii, 0.0, t, ALU.max, ALU.add)],
                        reads=[f"pb{bidx}", treg], writes=["pk"])
            else:
                hb = (j - 2) * 8
                out_v = V65[:, :].rearrange("p (h m) -> p h m", m=65)[:, hb:hb + 8, 0:64]
                in_v = bank[:, :].rearrange("p (h m) -> p h m", m=64)
                p.stage("dve", [lambda e, o=out_v, ii=in_v: e.tensor_copy(o, ii)],
                        reads=[f"pb{bidx}"], writes=["v65"])

        for half8 in range(2):
            fns = [lambda e, o=kv_pair_ap(pr), l=PK[:, pr * 128:(pr + 1) * 128],
                   r=V65[:, pr * 130:(pr + 1) * 130]:
                   e.matmul(o, l, r, start=True, stop=True)
                   for pr in range(half8 * 4, half8 * 4 + 4)]
            p.stage("pe", fns, reads=["pk", "v65"],
                    writes=sorted(set(kv_bank(pr) for pr in range(half8 * 4, half8 * 4 + 4))))
        for bi, (b0, wid) in enumerate(((0, 390), (390, 390), (780, 260))):
            bank = PB[KVBANKS[bi]]
            if i == 0:
                p.stage("dve", [lambda e, o=KVSB[:, b0:b0 + wid], ii=bank[:, 0:wid]:
                                e.tensor_copy(o, ii)],
                        reads=[f"pb{KVBANKS[bi]}"], writes=[f"kvsb{bi}"])
            else:
                p.stage("dve", [lambda e, o=KVSB[:, b0:b0 + wid], ii=bank[:, 0:wid]:
                                e.tensor_tensor(o, o, ii, ALU.add)],
                        reads=[f"pb{KVBANKS[bi]}", f"kvsb{bi}"], writes=[f"kvsb{bi}"])

        nxt = i + 4
        if nxt < 2 * NSUB:
            a = nxt if nxt < NSUB else nxt - NSUB
            dst = XT[:, (nxt % 4) * D:((nxt % 4) + 1) * D]
            p.stage("sync", [lambda e, o=dst, aa=a:
                             e.dma_start(o, d_x[aa * 128:(aa + 1) * 128, :])],
                    writes=[tx_reg[nxt % 4]], inc_sem=f"x{nxt % 4}")

        if i % 2 == 1:
            tile = i // 2
            gp_st.flush()
            sc_st.flush()
            for c in range(8):
                bidx = A_PROJ[proj_i[0] % 4]
                proj_i[0] += 1
                bank = PB[bidx]
                fns = []
                for k in range(8):
                    lhs = R[:, WQOFF + k * 3072 + c * 128: WQOFF + k * 3072 + (c + 1) * 128]
                    rhs = H1T[:, k * 256:(k + 1) * 256]
                    fns.append(lambda e, o=bank[:, 0:256], l=lhs, r=rhs,
                               st=(k == 0), sp=(k == 7): e.matmul(o, l, r, start=st, stop=sp))
                p.stage("pe", fns,
                        reads=["h1t"] + [f"wq{k}.{(c * 128) // 512}" for k in range(8)],
                        writes=[f"pb{bidx}"])
                ta = T1[:, (c % 2) * 512:(c % 2) * 512 + 256]
                tc = T1[:, (c % 2) * 512 + 256:(c % 2) * 512 + 512]
                tb2 = ATTN[:, (c % 2) * 512:(c % 2) * 512 + 256]
                treg, areg = f"t1.{c % 2}", f"attn.{c % 2}"
                p.stage("dve", [lambda e, o=ta, ii=bank[:, 0:256], bq=BQC[:, c:c + 1]:
                                e.tensor_scalar(o, ii, bq, 0.0, ALU.add, ALU.min),
                                lambda e, o=tc, ii=bank[:, 0:256], bq=BQC[:, c:c + 1]:
                                e.tensor_scalar(o, ii, bq, 0.0, ALU.add, ALU.max)],
                        reads=[f"pb{bidx}", "bqc"], writes=[treg])
                p.stage("act", [lambda e, o=tb2, ii=ta: e.activation(o, ii, AF.Exp)],
                        reads=[treg],
                        writes=[areg] + (["h1b"] if c % 2 == 0 else []))
                dst = PQT[:, c * S + tile * 256: c * S + tile * 256 + 256]
                p.stage("dve", [lambda e, o=dst, a=tc, b=tb2: e.tensor_add(o, a, b)],
                        reads=[treg, areg], writes=["pqt"])

    for pr in range(8 if a_subs == NSUB else 0):
        p.stage("dve", [lambda e, o=KVB[0:64, pr * 130: pr * 130 + 65],
                        ii=KVSB[0:64, pr * 130: pr * 130 + 65]: e.tensor_copy(o, ii)],
                reads=[f"kvsb{pr // 3}"], writes=["kvb"])
        p.stage("dve", [lambda e, o=KVB[64:128, pr * 130 + 65: pr * 130 + 130],
                        ii=KVSB[64:128, pr * 130 + 65: pr * 130 + 130]: e.tensor_copy(o, ii)],
                reads=[f"kvsb{pr // 3}"], writes=["kvb"])

    grep_regions = [f"kvsb{pr2}" for pr2 in range(3)]
    if bc_grps >= 0:
        p.stage("sync", [lambda e: e.dma_start(GREP, d_grep[:, :])],
                writes=grep_regions, inc_sem="sm2")

    gp_st.flush()
    sc_st.flush()

    pkv_bufs = [PK[:, :].bitcast(F32), V65[:, 0:1024].bitcast(F32)]
    w2r_gp = Streamer("gp", wstg_bufs, ["wstg0", "wstg1"], ["gb0", "gb1"], ["dve", "act"])
    w2r_gp.bi = gp_st.bi
    w2r_sc = Streamer("sync", pkv_bufs, ["pk", "v65"], ["kb0", "kb1"], ["act", "dve"])
    w2r_queue = list(w2r_pieces)
    if bc_grps >= 1:
        for _ in range(2):
            w2r_gp.dma(*w2_piece(*w2r_queue.pop(0)))
            w2r_sc.dma(*w2_piece(*w2r_queue.pop(0)))

    def bc_front(g):
        infos = []
        for s2 in range(2):
            gi = NSUB + g * 2 + s2
            tok0 = g * 256 + s2 * 128
            xa = XT[:, (gi % 4) * D:((gi % 4) + 1) * D]
            xreg = tx_reg[gi % 4]

            s1e = (2, 3) if g % 2 == 0 else (4, 5)
            at_banks = [(TRAB[0], "tra", (0, 1, 2)), (TRAB[1], "trb", (3, 4, 5)),
                        (PB[s1e[0]], f"pb{s1e[0]}", (6,)),
                        (PB[s1e[1]], f"pb{s1e[1]}", (7,))]
            for (bank, breg, prs) in at_banks:
                fns = [lambda e, o=bank[:, qq * 130:(qq + 1) * 130],
                       l=PQT[:, pr * S + tok0: pr * S + tok0 + 128],
                       r=KVB[:, pr * 130:(pr + 1) * 130]:
                       e.matmul(o, l, r, start=True, stop=True)
                       for qq, pr in enumerate(prs)]
                p.stage("pe", fns, reads=["pqt", "kvb"], writes=[breg])
            fns = []
            for (bank, breg, prs) in at_banks:
                nds = 2 * len(prs)
                den_s = DEN[:, 2 * prs[0]: 2 * prs[0] + nds]
                in_d = bank[:, 0:len(prs) * 130].rearrange(
                    "p (x m) -> p x m", m=65)[:, :, 64:65].squeeze(2)
                fns.append(lambda e, o=den_s, ii=in_d: e.tensor_copy(o, ii))
            p.stage("dve", fns,
                    reads=["tra", "trb", f"pb{s1e[0]}", f"pb{s1e[1]}"],
                    writes=["den", "zr"])
            p.stage("dve", [lambda e: e.tensor_single_scalar(DEN[:, :], DEN[:, :],
                                                             EPS_ATTN, ALU.add)],
                    reads=["den"], writes=["den"])
            p.stage("dve", [lambda e: e.reciprocal(DEN[:, :], DEN[:, :])],
                    reads=["den"], writes=["den", "zr"])
            fns = []
            for (bank, breg, prs) in at_banks:
                for qq, pr in enumerate(prs):
                    for hh2 in range(2):
                        hd = 2 * pr + hh2
                        srcp = bank[:, qq * 130 + hh2 * 65: qq * 130 + hh2 * 65 + 64]
                        fns.append(lambda e, o=ATTN[:, hd * 64:(hd + 1) * 64],
                                   ii=srcp, z=DEN[:, hd:hd + 1]:
                                   e.activation(o, ii, AF.Copy, bias=0.0, scale=z))
            p.stage("act", fns,
                    reads=["tra", "trb", f"pb{s1e[0]}", f"pb{s1e[1]}", "zr", "consts"],
                    writes=["attn.0", "attn.1", "h1b"])

            mean, rstd, rg = ln_stats(ATTN[:, :], ["attn.0", "attn.1"])
            p.stage("dve", [lambda e, o=ATTN[:, :], a=ATTN[:, :], r=rstd, m=mean:
                            e.tensor_scalar(o, a, m, r, ALU.subtract, ALU.mult)],
                    reads=["attn.0", "attn.1", rg],
                    writes=["attn.0", "attn.1", "h1b"])
            p.stage("dve", [lambda e, o=xa, a=xa, b=ATTN[:, :]: e.tensor_add(o, a, b)],
                    reads=["attn.0", "attn.1", xreg], writes=[xreg])

            mean2, rstd2, rg2 = ln_stats(xa, [xreg])
            p.stage("dve", [lambda e, o=H1B[:, :], a=xa, r=rstd2, m=mean2:
                            e.tensor_scalar(o, a, m, r, ALU.subtract, ALU.mult)],
                    reads=[xreg, rg2], writes=["h1b", "attn.0"])

            for hh in range(2):
                fns = []
                for b in range(4):
                    c = hh * 4 + b
                    fns.append(lambda e,
                               o=TRBF[hh][:, b * 128:(b + 1) * 128],
                               ii=H1B[:, c * 128:(c + 1) * 128]:
                               e.transpose(o, ii, IDENT[:, :]))
                p.stage("pe", fns, reads=["h1b", "ident"], writes=[TRREG[hh]])
                out3d = H1T[:, :].rearrange("p (c t) -> p c t", t=256)[:, hh * 4:hh * 4 + 4,
                                                                      s2 * 128:(s2 + 1) * 128]
                in3d = TRBF[hh][:, 0:512].rearrange("p (c t) -> p c t", t=128)
                p.stage("act", [lambda e, o=out3d, ii=in3d: e.copy(o, ii)],
                        reads=[TRREG[hh]], writes=["h1t"])
            infos.append(dict(xa=xa, xreg=xreg, gi=gi, tok0=tok0,
                              rstd2=rstd2, mean2=mean2, rg2=rg2))
        return infos

    def bc_ffn(g):
        s1e = (2, 3) if g % 2 == 0 else (4, 5)
        eob = [[0, 1], list(s1e)]

        def emit_w1(f):
            uh = f % 2
            ut = TRAB[uh][:, 0:256]
            fns = []
            for k in range(8):
                lhs = W1[:, k * FF + f * 128: k * FF + f * 128 + 128]
                rhs = H1T[:, k * 256: k * 256 + 256]
                fns.append(lambda e, o=ut, l=lhs, r=rhs, st=(k == 0), sp=(k == 7):
                           e.matmul(o, l, r, start=st, stop=sp))
            p.stage("pe", fns,
                    reads=["h1t"] + [f"w1p{k}.{(f * 128) // 512}" for k in range(8)],
                    writes=[TRREG[uh]])
            p.stage("act", [lambda e, o=UTC[:, uh * 256: uh * 256 + 256],
                            ii=ut, b=B1C[:, f:f + 1]:
                            e.activation(o, ii, AF.Relu, bias=b, scale=1.0)],
                    reads=[TRREG[uh], "b1c", "consts"], writes=[f"utc{uh}", "stg2b0"])

        eoregs = [f"pb{b}" for row in eob for b in row]

        def emit_w2(f):
            uh = f % 2
            fns = []
            for s2 in range(2):
                for nn2 in range(2):
                    eo = PB[eob[s2][nn2]]
                    lhs = UTC[:, uh * 256 + s2 * 128: uh * 256 + s2 * 128 + 128]
                    rhs = R[:, f * 1024 + nn2 * 512: f * 1024 + (nn2 + 1) * 512]
                    fns.append(lambda e, o=eo[:, :], l=lhs, r=rhs, st=(f == 0):
                               e.matmul(o, l, r, start=st, stop=False))
            for nn2 in (0, 1):
                wrec = p.reg.get(f"w2p{f}.{nn2}", {}).get("w")
                assert wrec is not None and wrec[0] in ("act", "dve")
            p.stage("pe", fns,
                    reads=[f"utc{uh}", f"w2p{f}.0", f"w2p{f}.1"],
                    writes=list(eoregs))

        for f in range(32):
            if g == 0 and w2r_queue:
                w2r_gp.dma(*w2_piece(*w2r_queue.pop(0)))
                w2r_sc.dma(*w2_piece(*w2r_queue.pop(0)))
            w2r_gp.flush(1)
            w2r_sc.flush(1)
            emit_w1(f)
            if f >= 1:
                emit_w2(f - 1)
        emit_w2(31)
        if g == 0:
            p.stage("sync", [lambda e: e.dma_start(BR1, d_brep[:, 0:512])],
                    writes=["stg2b0", "stg2b1", "brep"], inc_sem="sm")
            p.stage("sync", [lambda e: e.dma_start(BR2, d_brep[:, 512:768])],
                    writes=["stg2b1", "brep"], inc_sem="sm")
            p.stage("sync", [lambda e: e.dma_start(BR3, d_brep[:, 768:1024])],
                    writes=["pk", "v65", "brep"], inc_sem="sm")
            p.stage("sync", [lambda e: e.dma_start(B2ROW[0:1, :], d_b2[:, :])],
                    writes=["pk", "b2row"], inc_sem="sm")
        fns = []
        for s2 in range(2):
            for nn2 in range(2):
                eo = PB[eob[s2][nn2]]
                fns.append(lambda e, o=eo[:, :], r=B2ROW[0:1, nn2 * 512:(nn2 + 1) * 512]:
                           e.matmul(o, ONES[0:1, 0:128], r, start=False, stop=True))
        p.stage("pe", fns, reads=["ones", "b2row"], writes=list(eoregs))

    def bc_epi_y(g, sub_info):
        s1e = (2, 3) if g % 2 == 0 else (4, 5)
        for s2 in range(2):
            si = sub_info[s2]
            xa, xreg = si["xa"], si["xreg"]
            ybuf = WSTG if s2 == 0 else T1
            yregs = ["wstg0", "wstg1"] if s2 == 0 else ["t1.0", "t1.1"]
            banks = (0, 1) if s2 == 0 else s1e
            p.stage("dve", [lambda e, o=ybuf[:, :], a=xa, r=si["rstd2"],
                            m=si["mean2"]:
                            e.tensor_scalar(o, a, m, r, ALU.subtract, ALU.mult)],
                    reads=[xreg, si["rg2"]], writes=list(yregs))
            for nn2 in range(2):
                eo = PB[banks[nn2]]
                p.stage("dve", [lambda e, o=ybuf[:, nn2 * 512:(nn2 + 1) * 512],
                                a=ybuf[:, nn2 * 512:(nn2 + 1) * 512], b=eo[:, :]:
                                e.tensor_tensor(o, a, b, ALU.add)],
                        reads=[f"pb{banks[nn2]}", yregs[nn2]], writes=[yregs[nn2]])

    def bc_epi_rest(g, sub_info):
        for s2 in range(2):
            si = sub_info[s2]
            xa, xreg, gi, tok0 = si["xa"], si["xreg"], si["gi"], si["tok0"]
            ybuf = WSTG if s2 == 0 else T1
            yregs = ["wstg0", "wstg1"] if s2 == 0 else ["t1.0", "t1.1"]
            mean, rstd, rg = ln_stats(ybuf[:, :], yregs)
            p.stage("dve", [lambda e, o=ybuf[:, :], a=ybuf[:, :], r=rstd, m=mean:
                            e.tensor_scalar(o, a, m, r, ALU.subtract, ALU.mult)],
                    reads=yregs + [rg], writes=list(yregs))
            p.stage("dve", [lambda e, o=ybuf[:, :], a=ybuf[:, :]:
                            e.tensor_mul(o, a, GREP)],
                    reads=yregs + grep_regions, writes=list(yregs))
            p.stage("dve", [lambda e, o=ybuf[:, 0:512], a=ybuf[:, 0:512]:
                            e.tensor_add(o, a, BR1),
                            lambda e, o=ybuf[:, 512:768], a=ybuf[:, 512:768]:
                            e.tensor_add(o, a, BR2),
                            lambda e, o=ybuf[:, 768:1024], a=ybuf[:, 768:1024]:
                            e.tensor_add(o, a, BR3)],
                    reads=yregs + ["brep"], writes=list(yregs))
            mean3, rstd3, rg3 = ln_stats(ybuf[:, :], yregs)
            p.stage("dve", [lambda e, o=ybuf[:, :], a=ybuf[:, :], r=rstd3, m=mean3:
                            e.tensor_scalar(o, a, m, r, ALU.subtract, ALU.mult)],
                    reads=yregs + [rg3], writes=list(yregs))
            p.stage("dve", [lambda e, o=xa, a=xa, b=ybuf[:, :]: e.tensor_add(o, a, b)],
                    reads=yregs + [xreg], writes=[xreg])
            p.stage("sync", [lambda e, o=d_out[tok0:tok0 + 128, :], ii=xa:
                             e.dma_start(o, ii)],
                    reads=[xreg], inc_sem=f"o{gi % 4}")
            nxt = gi + 4
            if nxt < 2 * NSUB:
                a = nxt - NSUB
                dst = XT[:, (nxt % 4) * D:((nxt % 4) + 1) * D]
                p.stage("sync", [lambda e, o=dst, aa=a:
                                 e.dma_start(o, d_x[aa * 128:(aa + 1) * 128, :])],
                        writes=[tx_reg[nxt % 4]], inc_sem=f"x{nxt % 4}")

    infos = bc_front(0) if bc_grps >= 1 else None
    prev_infos = None
    for g in range(max(bc_grps, 0)):
        bc_ffn(g)
        if prev_infos is not None:
            bc_epi_rest(g - 1, prev_infos)
        bc_epi_y(g, infos)
        nxt_infos = bc_front(g + 1) if g + 1 < bc_grps else None
        prev_infos = infos
        infos = nxt_infos
    if prev_infos is not None:
        bc_epi_rest(bc_grps - 1, prev_infos)

    w2r_gp.flush()
    w2r_sc.flush()
    for osl in range(4):
        if p.cnt[f"o{osl}"]:
            p.raw_wait("sync", (f"o{osl}", p.cnt[f"o{osl}"]))
    if trunc is not None:
        for sname in ("pe", "act", "dve"):
            if p.cnt[sname]:
                p.raw_wait("sync", (sname, p.cnt[sname]))

    with stack:
        sems = {}
        for sname in sorted(p.cnt.keys()):
            sems[sname] = stack.enter_context(nc.semaphore(sname))
        block = stack.enter_context(nc.Block())

        def emitter(key):
            def run(eng):
                for (waits, fn, inc) in p.items[key]:
                    for (s, v) in waits:
                        eng.wait_ge(sems[s], v)
                    if fn is None:
                        continue
                    ins = fn(eng)
                    if inc is not None:
                        ins.then_inc(sems[inc[0]], inc[1])
            return run

        block.tensor(emitter("pe"))
        block.scalar(emitter("act"))
        block.vector(emitter("dve"))
        block.sync(emitter("sync"))
        block.gpsimd(emitter("gp"))

    return nc


_NC_CACHE = []


def kernel(L0_attn_x, L0_attn_W_qkv, L0_attn_b_qkv, L0_attn_W_proj, L0_attn_b_proj,
           L0_ffn_router_w, L0_ffn_router_b, L0_ffn_w1, L0_ffn_b1, L0_ffn_w2,
           L0_ffn_b2, L0_ffn_gamma, L0_ffn_beta):
    x = np.ascontiguousarray(np.asarray(L0_attn_x, dtype=np.float32))
    wqkv = np.ascontiguousarray(np.asarray(L0_attn_W_qkv, dtype=np.float32))
    bqkv = np.asarray(L0_attn_b_qkv, dtype=np.float32)
    w1 = np.ascontiguousarray(np.asarray(L0_ffn_w1, dtype=np.float32)[0])
    b1 = np.asarray(L0_ffn_b1, dtype=np.float32)[0]
    w2 = np.ascontiguousarray(np.asarray(L0_ffn_w2, dtype=np.float32)[0])
    b2 = np.asarray(L0_ffn_b2, dtype=np.float32)[0]
    gamma = np.asarray(L0_ffn_gamma, dtype=np.float32)
    beta = np.asarray(L0_ffn_beta, dtype=np.float32)

    bkv = np.ascontiguousarray(bqkv[D:3 * D][None, :].astype(BFnp))
    bqc = np.ascontiguousarray(bqkv[0:D].reshape(8, 128).T)
    b1c = np.ascontiguousarray(b1.reshape(32, 128).T)
    b2row = np.ascontiguousarray(b2[None, :].astype(BFnp))
    grep = np.ascontiguousarray(np.broadcast_to(gamma[None, :], (128, D)))
    brep = np.ascontiguousarray(np.broadcast_to(beta[None, :], (128, D)))
    ident = np.eye(128, dtype=BFnp)
    consts = np.ascontiguousarray(
        np.tile(np.array([[EPS_LN, 0.0, 1.0, EPS_ATTN]], dtype=np.float32), (128, 1)))

    if not _NC_CACHE:
        _NC_CACHE.append(build_program())
    nc = _NC_CACHE[0]
    in_maps = []
    for c in range(8):
        in_maps.append({
            "x": np.ascontiguousarray(x[c]),
            "wqkv": wqkv, "bkv": bkv, "bqc": bqc,
            "w1": w1, "b1c": b1c, "w2": w2, "b2row": b2row,
            "grep": grep, "brep": brep, "ident": ident, "consts": consts,
        })
    res = run_bass_kernel_spmd(nc, in_maps, core_ids=list(range(8)))
    out = np.stack([np.asarray(res.results[c]["out"]) for c in range(8)], axis=0)
    return out.astype(np.float32)


# version 30
# speedup vs baseline: 1.0898x; 1.0898x over previous
import numpy as np
import ml_dtypes
from collections import defaultdict
from contextlib import ExitStack

import concourse.bass as bass
import concourse.mybir as mybir
from concourse.bass_utils import run_bass_kernel_spmd

F32 = mybir.dt.float32
BF16 = mybir.dt.bfloat16
AF = mybir.ActivationFunctionType
ALU = mybir.AluOpType
BFnp = ml_dtypes.bfloat16

S = 2048
D = 1024
H = 16
FF = 4096
NSUB = S // 128
NGRP = 8
EPS_LN = 1e-5
EPS_ATTN = 1e-6

ENGS = ("pe", "act", "dve", "sync", "gp")
ESEM = {"pe": "pe", "act": "act", "dve": "dve", "sync": "sm", "gp": "gb0"}
COMPUTE_SEMS = ("pe", "act", "dve", "gpc")


class Plan:

    def __init__(self):
        self.items = {e: [] for e in ENGS}
        self.cnt = defaultdict(int)
        self.lastw = {e: {} for e in ENGS}
        self.reg = {}
        self.nstages = 0
        self.limit = None

    def _emit(self, eng, fn, waits, inc, inc_sem):
        merged = {}
        for t in waits:
            if t is None:
                continue
            s, v = t
            if v > merged.get(s, -1):
                merged[s] = v
        wl = []
        for s, v in merged.items():
            if v > self.lastw[eng].get(s, -1):
                wl.append((s, v))
                self.lastw[eng][s] = v
        tok = None
        rec = None
        if inc:
            sem = inc_sem or ESEM[eng]
            amt = 1 if sem in COMPUTE_SEMS else 16
            self.cnt[sem] += amt
            tok = (sem, self.cnt[sem])
            rec = (sem, amt)
        self.items[eng].append((wl, fn, rec))
        return tok

    def stage(self, eng, fns, reads=(), writes=(), extra=(), inc_sem=None):
        self.nstages += 1
        if self.limit is not None and self.nstages > self.limit:
            return ("sm", 0)
        waits = list(extra)
        for r in reads:
            rec = self.reg.get(r)
            if rec and rec["w"] is not None:
                waits.append(rec["w"])
        for wr in writes:
            rec = self.reg.get(wr)
            if rec:
                if rec["w"] is not None:
                    waits.append(rec["w"])
                waits.extend(rec["r"])
        tok = None
        n = len(fns)
        for k, fn in enumerate(fns):
            tok = self._emit(eng, fn, waits if k == 0 else (),
                             inc=(k == n - 1), inc_sem=inc_sem)
        for r in reads:
            self.reg.setdefault(r, {"w": None, "r": []})["r"].append(tok)
        for wr in writes:
            self.reg[wr] = {"w": tok, "r": []}
        return tok

    def set_writer(self, region, tok):
        self.reg[region] = {"w": tok, "r": []}

    def raw_wait(self, eng, tok):
        self.items[eng].append(([tok], None, None))


def build_program(trunc=None, stage_limit=None):

    nc = bass.Bass("TRN2", target_bir_lowering=False, debug=False)

    d_x = nc.dram_tensor("x", [S, D], F32, kind="ExternalInput")
    d_wqkv = nc.dram_tensor("wqkv", [D, 3 * D], F32, kind="ExternalInput")
    d_bkv = nc.dram_tensor("bkv", [1, 2 * D], BF16, kind="ExternalInput")
    d_bqc = nc.dram_tensor("bqc", [128, 8], F32, kind="ExternalInput")
    d_w1 = nc.dram_tensor("w1", [D, FF], F32, kind="ExternalInput")
    d_b1c = nc.dram_tensor("b1c", [128, 32], F32, kind="ExternalInput")
    d_w2 = nc.dram_tensor("w2", [FF, D], F32, kind="ExternalInput")
    d_b2 = nc.dram_tensor("b2row", [1, D], BF16, kind="ExternalInput")
    d_grep = nc.dram_tensor("grep", [128, D], F32, kind="ExternalInput")
    d_brep = nc.dram_tensor("brep", [128, D], F32, kind="ExternalInput")
    d_id = nc.dram_tensor("ident", [128, 128], BF16, kind="ExternalInput")
    d_const = nc.dram_tensor("consts", [128, 4], F32, kind="ExternalInput")
    d_out = nc.dram_tensor("out", [S, D], F32, kind="ExternalOutput")

    p = Plan()
    p.limit = stage_limit
    stack = ExitStack()
    sb = lambda n, shp, dt: stack.enter_context(nc.sbuf_tensor(n, shp, dt))
    psa = lambda n, shp, dt: stack.enter_context(nc.psum_tensor(n, shp, dt))

    R = sb("R", [128, 32768], BF16)
    WQOFF = 8192
    W1 = sb("W1", [128, 8 * FF], BF16)
    PQT = sb("PQT", [128, 8 * S], BF16)
    H1T = sb("H1T", [128, 8 * 256], BF16)
    XT = sb("XT", [128, 4 * D], F32)
    PK = sb("PK", [128, D], BF16)
    V65 = sb("V65", [128, H * 65], BF16)
    ATTN = sb("ATTN", [128, D], F32)
    H1B = ATTN[:, 0:512].bitcast(BF16)
    STG2 = sb("STG2", [128, D], F32)
    T1 = sb("T1", [128, D], F32)
    WSTG = sb("WSTG", [128, D], F32)
    KVB = sb("KVB", [128, 8 * 130], BF16)
    KVSB = sb("KVSB", [128, 8 * 130], F32)
    GREP = KVSB[:, 0:1024]
    BQC = sb("BQC", [128, 8], F32)
    B1C = sb("B1C", [128, 32], F32)
    IDENT = sb("IDENT", [128, 128], BF16)
    ONES = sb("ONES", [1, 128], BF16)
    SCR = sb("SCR", [128, 14 * 12], F32)
    CONSTS = sb("CONSTS", [128, 4], F32)
    DEN = sb("DEN", [128, H], F32)
    UTC = STG2[:, 0:256].bitcast(BF16)
    BR1 = STG2[:, 256:768]
    BR2 = STG2[:, 768:1024]
    BR3 = V65[:, 0:1024].bitcast(F32)[:, 0:256]
    B2ROW = PK[:, :]
    BKV = R[0:1, 6144:8192]

    nc.const_aps.register(F32, EPS_LN, CONSTS[:, 0:1])
    nc.const_aps.register(F32, 0.0, CONSTS[:, 1:2])
    nc.const_aps.register(F32, 1.0, CONSTS[:, 2:3])
    nc.const_aps.register(F32, EPS_ATTN, CONSTS[:, 3:4])

    PB = [psa(f"pb{i}", [128, 512], F32) for i in range(6)]
    TRAB = [psa("tra", [128, 512], F32), psa("trb", [128, 512], F32)]
    TRBF = [t[:, :].bitcast(BF16) for t in TRAB]
    TRREG = ["tra", "trb"]
    A_PROJ = [0, 1, 5, 2]
    KVBANKS = [2, 3, 4]
    EOB = [[0, 1], [2, 3]]

    def kv_pair_ap(pr):
        return PB[KVBANKS[pr // 3]][:, (pr % 3) * 130:((pr % 3) + 1) * 130]

    def kv_bank(pr):
        return f"pb{KVBANKS[pr // 3]}"

    ln_id = [0]

    def ln_stats(src_ap, src_regions):
        slot = ln_id[0] % 12
        ln_id[0] += 1
        off = slot * 14
        st = SCR[:, off:off + 12]
        mv = SCR[:, off + 12:off + 14]
        rstd = mv[:, 1:2]
        rg = f"scr{slot}"
        p.stage("dve", [lambda e, a=st, b=src_ap: e.bn_stats(a[:, 0:6], b[:, 0:512]),
                        lambda e, a=st, b=src_ap: e.bn_stats(a[:, 6:12], b[:, 512:1024])],
                reads=list(src_regions), writes=[rg])
        p.stage("dve", [lambda e, a=mv, b=st: e.bn_aggr(a, b)], reads=[rg], writes=[rg])
        p.stage("act", [lambda e, a=rstd, b=mv: e.activation(a, b[:, 1:2], AF.Sqrt,
                                                           bias=EPS_LN, scale=1.0)],
                reads=[rg, "consts"], writes=[rg])
        p.stage("dve", [lambda e, a=rstd: e.reciprocal(a, a)], reads=[rg], writes=[rg])
        return mv[:, 0:1], rstd, rg

    class Streamer:

        def __init__(self, qe, bufs, buf_regions, buf_sems, cast_engs):
            self.qe = qe
            self.bufs = bufs
            self.bregs = buf_regions
            self.bsems = buf_sems
            self.cast_engs = cast_engs
            self.pending = []
            self.ci = 0
            self.bi = 0

        def dma(self, dram_ap, dest_ap, dest_regions, extra=()):
            b = self.bi % len(self.bufs)
            self.bi += 1
            while any(pb == b for (pb, _, _) in self.pending):
                self.flush(1)
            p.stage(self.qe, [lambda e, o=self.bufs[b], i=dram_ap: e.dma_start(o, i)],
                    writes=[self.bregs[b]], extra=list(extra), inc_sem=self.bsems[b])
            self.pending.append((b, dest_ap, dest_regions))

        def flush(self, n=10 ** 9):
            done = 0
            while self.pending and done < n:
                b, dest, dregs = self.pending.pop(0)
                eng = self.cast_engs[self.ci % len(self.cast_engs)]
                self.ci += 1
                if eng == "act":
                    fn = lambda e, o=dest, i=self.bufs[b]: e.activation(o, i, AF.Copy)
                else:
                    fn = lambda e, o=dest, i=self.bufs[b]: e.tensor_copy(o, i)
                p.stage(eng, [fn], reads=[self.bregs[b]], writes=list(dregs))
                done += 1

    wstg_bufs = [WSTG[:, i * 512:(i + 1) * 512] for i in range(2)]
    stg2_bufs = [STG2[:, i * 512:(i + 1) * 512] for i in range(2)]
    gp_st = Streamer("gp", wstg_bufs, ["wstg0", "wstg1"], ["gb0", "gb1"], ["act", "dve"])
    sc_st = Streamer("sync", stg2_bufs, ["stg2b0", "stg2b1"], ["sb0", "sb1"], ["dve", "act"])

    def wq_piece(k, c2):
        return (d_wqkv[k * 128:(k + 1) * 128, c2 * 512:(c2 + 1) * 512],
                R[:, WQOFF + k * 3072 + c2 * 512: WQOFF + k * 3072 + (c2 + 1) * 512],
                [f"wq{k}.{c2}"])

    def w1_piece(k, j):
        return (d_w1[k * 128:(k + 1) * 128, j * 512:(j + 1) * 512],
                W1[:, k * FF + j * 512: k * FF + (j + 1) * 512], [f"w1p{k}.{j}"])

    def w2_piece(f, nn):
        regs = [f"w2p{f}.{nn}"]
        addr = f * 1024 + nn * 512 - WQOFF
        if addr >= 0:
            k = addr // 3072
            c2 = (addr % 3072) // 512
            regs.append(f"wq{k}.{c2}")
        return (d_w2[f * 128:(f + 1) * 128, nn * 512:(nn + 1) * 512],
                R[:, f * 1024 + nn * 512: f * 1024 + (nn + 1) * 512], regs)

    smalls = [
        (IDENT[:, :], d_id[:, :], "ident"),
        (CONSTS[:, :], d_const[:, :], "consts"),
        (BKV, d_bkv[:, :], "bkv"),
        (BQC[:, :], d_bqc[:, :], "bqc"),
        (B1C[:, :], d_b1c[:, :], "b1c"),
    ]
    for (dst, srcd, rg) in smalls:
        p.stage("sync", [lambda e, o=dst, i=srcd: e.dma_start(o, i)],
                writes=[rg], inc_sem="sm")
    sm_all = ("sm", p.cnt["sm"])
    for (_, _, rg) in smalls:
        p.set_writer(rg, sm_all)
    bkv_regions = [f"w2p{fk}.{nn}" for fk in (6, 7) for nn in (0, 1)]
    for rg in bkv_regions:
        p.set_writer(rg, sm_all)

    p.stage("dve", [lambda e: e.memset(ONES[:, :], 1.0)], writes=["ones"])
    p.stage("dve", [lambda e: e.memset(V65[:, :], 1.0)], writes=["v65"])
    p.stage("dve", [lambda e: e.memset(KVB[:, :], 0.0)], writes=["kvb"])

    tx_reg = ["xt0", "xt1", "xt2", "xt3"]
    for sl in range(4):
        p.stage("sync", [lambda e, s=sl: e.dma_start(
            XT[:, s * D:(s + 1) * D], d_x[s * 128:(s + 1) * 128, :])],
                writes=[tx_reg[sl]], inc_sem=f"x{sl}")

    kv_pieces = [(k, c2) for c2 in (2, 3, 4, 5) for k in range(8)]
    q_pieces = [(k, c2) for c2 in (0, 1) for k in range(8)]
    w1_pieces = [(k, j) for j in range(8) for k in range(8)]
    w2e_pieces = [(f, n) for f in range(6) for n in range(2)]
    w2r_pieces = [(f, n) for f in range(6, 32) for n in range(2)]

    nbatches = 4 if trunc is None or trunc[0] != "P" else trunc[1]
    for r in range(nbatches):
        for t in range(4):
            gp_st.dma(*wq_piece(*kv_pieces[r * 8 + t]))
            sc_st.dma(*wq_piece(*kv_pieces[r * 8 + 4 + t]))
        gp_st.flush()
        sc_st.flush()

    proj_i = [0]
    a_subs = NSUB if trunc is None or trunc[0] != "A" else trunc[1]
    bc_grps = NGRP if trunc is None or trunc[0] != "BC" else trunc[1]
    if trunc is not None and trunc[0] in ("A", "P"):
        bc_grps = -1
        if trunc[0] == "P":
            a_subs = 0
    for i in range(a_subs):
        half = i % 2
        slot = i % 4
        xa = XT[:, slot * D:(slot + 1) * D]
        xreg = tx_reg[slot]

        gp_st.flush()
        sc_st.flush()
        if i < 2:
            for t in range(4):
                gp_st.dma(*wq_piece(*q_pieces[i * 8 + t]))
                sc_st.dma(*wq_piece(*q_pieces[i * 8 + 4 + t]))
        elif i < 13:
            for t in range(3):
                idx = (i - 2) * 6 + t * 2
                if idx < 64:
                    gp_st.dma(*w1_piece(*w1_pieces[idx]))
                if idx + 1 < 64:
                    sc_st.dma(*w1_piece(*w1_pieces[idx + 1]))
        if 8 <= i < 14:
            gp_st.dma(*w2_piece(*w2e_pieces[(i - 8) * 2]))
            sc_st.dma(*w2_piece(*w2e_pieces[(i - 8) * 2 + 1]))

        mean, rstd, rg = ln_stats(xa, [xreg])
        p.stage("dve", [lambda e, o=H1B[:, :], a=xa, r=rstd, m=mean:
                        e.tensor_scalar(o, a, m, r, ALU.subtract, ALU.mult)],
                reads=[xreg, rg], writes=["h1b", "attn.0"])

        for hh in range(2):
            fns = []
            for b in range(4):
                c = hh * 4 + b
                fns.append(lambda e, o=TRBF[hh][:, b * 128:(b + 1) * 128],
                           ii=H1B[:, c * 128:(c + 1) * 128]: e.transpose(o, ii, IDENT[:, :]))
            p.stage("pe", fns, reads=["h1b", "ident"], writes=[TRREG[hh]])
            out3d = H1T[:, :].rearrange("p (c t) -> p c t", t=256)[:, hh * 4:hh * 4 + 4,
                                                                  half * 128:(half + 1) * 128]
            in3d = TRBF[hh][:, 0:512].rearrange("p (c t) -> p c t", t=128)
            p.stage("act", [lambda e, o=out3d, ii=in3d: e.copy(o, ii)],
                    reads=[TRREG[hh]], writes=["h1t"])

        for j in range(4):
            bidx = A_PROJ[proj_i[0] % 4]
            proj_i[0] += 1
            bank = PB[bidx]
            c2 = 2 + j
            fns = []
            for k in range(8):
                lhs = H1T[:, k * 256 + half * 128: k * 256 + (half + 1) * 128]
                rhs = R[:, WQOFF + k * 3072 + c2 * 512: WQOFF + k * 3072 + (c2 + 1) * 512]
                fns.append(lambda e, o=bank[:, :], l=lhs, r=rhs, st=(k == 0):
                           e.matmul(o, l, r, start=st, stop=False))
            fns.append(lambda e, o=bank[:, :], r=BKV[0:1, j * 512:(j + 1) * 512]:
                       e.matmul(o, ONES[0:1, 0:128], r, start=False, stop=True))
            p.stage("pe", fns,
                    reads=["h1t", "ones"] + bkv_regions + [f"wq{k}.{c2}" for k in range(8)],
                    writes=[f"pb{bidx}"])
            if j < 2:
                tb = T1[:, (j % 2) * 512:(j % 2) * 512 + 512]
                treg = f"t1.{j % 2}"
                tw = [treg]
                p.stage("dve", [lambda e, o=tb, ii=bank[:, :]:
                                e.tensor_single_scalar(o, ii, 0.0, ALU.min)],
                        reads=[f"pb{bidx}"], writes=tw)
                p.stage("act", [lambda e, o=tb, ii=tb: e.activation(o, ii, AF.Exp)],
                        reads=[treg], writes=tw)
                p.stage("dve", [lambda e, o=PK[:, j * 512:(j + 1) * 512], ii=bank[:, :], t=tb:
                                e.scalar_tensor_tensor(o, ii, 0.0, t, ALU.max, ALU.add)],
                        reads=[f"pb{bidx}", treg], writes=["pk"])
            else:
                hb = (j - 2) * 8
                out_v = V65[:, :].rearrange("p (h m) -> p h m", m=65)[:, hb:hb + 8, 0:64]
                in_v = bank[:, :].rearrange("p (h m) -> p h m", m=64)
                p.stage("dve", [lambda e, o=out_v, ii=in_v: e.tensor_copy(o, ii)],
                        reads=[f"pb{bidx}"], writes=["v65"])

        for half8 in range(2):
            fns = [lambda e, o=kv_pair_ap(pr), l=PK[:, pr * 128:(pr + 1) * 128],
                   r=V65[:, pr * 130:(pr + 1) * 130]:
                   e.matmul(o, l, r, start=True, stop=True)
                   for pr in range(half8 * 4, half8 * 4 + 4)]
            p.stage("pe", fns, reads=["pk", "v65"],
                    writes=sorted(set(kv_bank(pr) for pr in range(half8 * 4, half8 * 4 + 4))))
        for bi, (b0, wid) in enumerate(((0, 390), (390, 390), (780, 260))):
            bank = PB[KVBANKS[bi]]
            if i == 0:
                p.stage("dve", [lambda e, o=KVSB[:, b0:b0 + wid], ii=bank[:, 0:wid]:
                                e.tensor_copy(o, ii)],
                        reads=[f"pb{KVBANKS[bi]}"], writes=[f"kvsb{bi}"])
            else:
                p.stage("dve", [lambda e, o=KVSB[:, b0:b0 + wid], ii=bank[:, 0:wid]:
                                e.tensor_tensor(o, o, ii, ALU.add)],
                        reads=[f"pb{KVBANKS[bi]}", f"kvsb{bi}"], writes=[f"kvsb{bi}"])

        nxt = i + 4
        if nxt < 2 * NSUB:
            a = nxt if nxt < NSUB else nxt - NSUB
            dst = XT[:, (nxt % 4) * D:((nxt % 4) + 1) * D]
            p.stage("sync", [lambda e, o=dst, aa=a:
                             e.dma_start(o, d_x[aa * 128:(aa + 1) * 128, :])],
                    writes=[tx_reg[nxt % 4]], inc_sem=f"x{nxt % 4}")

        if i % 2 == 1:
            tile = i // 2
            gp_st.flush()
            sc_st.flush()
            for c in range(8):
                bidx = A_PROJ[proj_i[0] % 4]
                proj_i[0] += 1
                bank = PB[bidx]
                fns = []
                for k in range(8):
                    lhs = R[:, WQOFF + k * 3072 + c * 128: WQOFF + k * 3072 + (c + 1) * 128]
                    rhs = H1T[:, k * 256:(k + 1) * 256]
                    fns.append(lambda e, o=bank[:, 0:256], l=lhs, r=rhs,
                               st=(k == 0), sp=(k == 7): e.matmul(o, l, r, start=st, stop=sp))
                p.stage("pe", fns,
                        reads=["h1t"] + [f"wq{k}.{(c * 128) // 512}" for k in range(8)],
                        writes=[f"pb{bidx}"])
                ta = T1[:, (c % 2) * 512:(c % 2) * 512 + 256]
                tc = T1[:, (c % 2) * 512 + 256:(c % 2) * 512 + 512]
                tb2 = ATTN[:, (c % 2) * 512:(c % 2) * 512 + 256]
                treg, areg = f"t1.{c % 2}", f"attn.{c % 2}"
                p.stage("dve", [lambda e, o=ta, ii=bank[:, 0:256], bq=BQC[:, c:c + 1]:
                                e.tensor_scalar(o, ii, bq, 0.0, ALU.add, ALU.min),
                                lambda e, o=tc, ii=bank[:, 0:256], bq=BQC[:, c:c + 1]:
                                e.tensor_scalar(o, ii, bq, 0.0, ALU.add, ALU.max)],
                        reads=[f"pb{bidx}", "bqc"], writes=[treg])
                p.stage("act", [lambda e, o=tb2, ii=ta: e.activation(o, ii, AF.Exp)],
                        reads=[treg],
                        writes=[areg] + (["h1b"] if c % 2 == 0 else []))
                dst = PQT[:, c * S + tile * 256: c * S + tile * 256 + 256]
                p.stage("dve", [lambda e, o=dst, a=tc, b=tb2: e.tensor_add(o, a, b)],
                        reads=[treg, areg], writes=["pqt"])

    for pr in range(8 if a_subs == NSUB else 0):
        p.stage("dve", [lambda e, o=KVB[0:64, pr * 130: pr * 130 + 65],
                        ii=KVSB[0:64, pr * 130: pr * 130 + 65]: e.tensor_copy(o, ii)],
                reads=[f"kvsb{pr // 3}"], writes=["kvb"])
        p.stage("dve", [lambda e, o=KVB[64:128, pr * 130 + 65: pr * 130 + 130],
                        ii=KVSB[64:128, pr * 130 + 65: pr * 130 + 130]: e.tensor_copy(o, ii)],
                reads=[f"kvsb{pr // 3}"], writes=["kvb"])

    grep_regions = [f"kvsb{pr2}" for pr2 in range(3)]
    if bc_grps >= 0:
        p.stage("sync", [lambda e: e.dma_start(GREP, d_grep[:, :])],
                writes=grep_regions, inc_sem="sm2")

    gp_st.flush()
    sc_st.flush()

    pkv_bufs = [PK[:, :].bitcast(F32), V65[:, 0:1024].bitcast(F32)]
    w2r_gp = Streamer("gp", wstg_bufs, ["wstg0", "wstg1"], ["gb0", "gb1"], ["dve", "act"])
    w2r_gp.bi = gp_st.bi
    w2r_sc = Streamer("sync", pkv_bufs, ["pk", "v65"], ["kb0", "kb1"], ["act", "dve"])
    w2r_queue = list(w2r_pieces)
    if bc_grps >= 1:
        for _ in range(2):
            w2r_gp.dma(*w2_piece(*w2r_queue.pop(0)))
            w2r_sc.dma(*w2_piece(*w2r_queue.pop(0)))

    def bc_front(g):
        infos = []
        for s2 in range(2):
            gi = NSUB + g * 2 + s2
            tok0 = g * 256 + s2 * 128
            xa = XT[:, (gi % 4) * D:((gi % 4) + 1) * D]
            xreg = tx_reg[gi % 4]

            s1e = (2, 3) if g % 2 == 0 else (4, 5)
            at_banks = [(TRAB[0], "tra", (0, 1, 2)), (TRAB[1], "trb", (3, 4, 5)),
                        (PB[s1e[0]], f"pb{s1e[0]}", (6,)),
                        (PB[s1e[1]], f"pb{s1e[1]}", (7,))]
            for (bank, breg, prs) in at_banks:
                fns = [lambda e, o=bank[:, qq * 130:(qq + 1) * 130],
                       l=PQT[:, pr * S + tok0: pr * S + tok0 + 128],
                       r=KVB[:, pr * 130:(pr + 1) * 130]:
                       e.matmul(o, l, r, start=True, stop=True)
                       for qq, pr in enumerate(prs)]
                p.stage("pe", fns, reads=["pqt", "kvb"], writes=[breg])
            fns = []
            for (bank, breg, prs) in at_banks:
                nds = 2 * len(prs)
                den_s = DEN[:, 2 * prs[0]: 2 * prs[0] + nds]
                in_d = bank[:, 0:len(prs) * 130].rearrange(
                    "p (x m) -> p x m", m=65)[:, :, 64:65].squeeze(2)
                fns.append(lambda e, o=den_s, ii=in_d: e.tensor_copy(o, ii))
            p.stage("dve", fns,
                    reads=["tra", "trb", f"pb{s1e[0]}", f"pb{s1e[1]}"],
                    writes=["den", "zr"])
            p.stage("dve", [lambda e: e.tensor_single_scalar(DEN[:, :], DEN[:, :],
                                                             EPS_ATTN, ALU.add)],
                    reads=["den"], writes=["den"])
            p.stage("dve", [lambda e: e.reciprocal(DEN[:, :], DEN[:, :])],
                    reads=["den"], writes=["den", "zr"])
            fns = []
            for (bank, breg, prs) in at_banks:
                for qq, pr in enumerate(prs):
                    for hh2 in range(2):
                        hd = 2 * pr + hh2
                        srcp = bank[:, qq * 130 + hh2 * 65: qq * 130 + hh2 * 65 + 64]
                        fns.append(lambda e, o=ATTN[:, hd * 64:(hd + 1) * 64],
                                   ii=srcp, z=DEN[:, hd:hd + 1]:
                                   e.activation(o, ii, AF.Copy, bias=0.0, scale=z))
            p.stage("act", fns,
                    reads=["tra", "trb", f"pb{s1e[0]}", f"pb{s1e[1]}", "zr", "consts"],
                    writes=["attn.0", "attn.1", "h1b"])

            mean, rstd, rg = ln_stats(ATTN[:, :], ["attn.0", "attn.1"])
            p.stage("dve", [lambda e, o=ATTN[:, :], a=ATTN[:, :], r=rstd, m=mean:
                            e.tensor_scalar(o, a, m, r, ALU.subtract, ALU.mult)],
                    reads=["attn.0", "attn.1", rg],
                    writes=["attn.0", "attn.1", "h1b"])
            p.stage("dve", [lambda e, o=xa, a=xa, b=ATTN[:, :]: e.tensor_add(o, a, b)],
                    reads=["attn.0", "attn.1", xreg], writes=[xreg])

            mean2, rstd2, rg2 = ln_stats(xa, [xreg])
            p.stage("dve", [lambda e, o=H1B[:, :], a=xa, r=rstd2, m=mean2:
                            e.tensor_scalar(o, a, m, r, ALU.subtract, ALU.mult)],
                    reads=[xreg, rg2], writes=["h1b", "attn.0"])

            for hh in range(2):
                fns = []
                for b in range(4):
                    c = hh * 4 + b
                    fns.append(lambda e,
                               o=TRBF[hh][:, b * 128:(b + 1) * 128],
                               ii=H1B[:, c * 128:(c + 1) * 128]:
                               e.transpose(o, ii, IDENT[:, :]))
                p.stage("pe", fns, reads=["h1b", "ident"], writes=[TRREG[hh]])
                out3d = H1T[:, :].rearrange("p (c t) -> p c t", t=256)[:, hh * 4:hh * 4 + 4,
                                                                      s2 * 128:(s2 + 1) * 128]
                in3d = TRBF[hh][:, 0:512].rearrange("p (c t) -> p c t", t=128)
                p.stage("act", [lambda e, o=out3d, ii=in3d: e.copy(o, ii)],
                        reads=[TRREG[hh]], writes=["h1t"])
            infos.append(dict(xa=xa, xreg=xreg, gi=gi, tok0=tok0,
                              rstd2=rstd2, mean2=mean2, rg2=rg2))
        return infos

    def bc_ffn(g, epi_chunks=()):
        s1e = (2, 3) if g % 2 == 0 else (4, 5)
        eob = [[0, 1], list(s1e)]

        def emit_w1(f):
            uh = f % 2
            ut = TRAB[uh][:, 0:256]
            fns = []
            for k in range(8):
                lhs = W1[:, k * FF + f * 128: k * FF + f * 128 + 128]
                rhs = H1T[:, k * 256: k * 256 + 256]
                fns.append(lambda e, o=ut, l=lhs, r=rhs, st=(k == 0), sp=(k == 7):
                           e.matmul(o, l, r, start=st, stop=sp))
            p.stage("pe", fns,
                    reads=["h1t"] + [f"w1p{k}.{(f * 128) // 512}" for k in range(8)],
                    writes=[TRREG[uh]])
            p.stage("act", [lambda e, o=UTC[:, uh * 256: uh * 256 + 256],
                            ii=ut, b=B1C[:, f:f + 1]:
                            e.activation(o, ii, AF.Relu, bias=b, scale=1.0)],
                    reads=[TRREG[uh], "b1c", "consts"], writes=[f"utc{uh}", "stg2b0"])

        eoregs = [f"pb{b}" for row in eob for b in row]

        def emit_w2(f):
            uh = f % 2
            fns = []
            for s2 in range(2):
                for nn2 in range(2):
                    eo = PB[eob[s2][nn2]]
                    lhs = UTC[:, uh * 256 + s2 * 128: uh * 256 + s2 * 128 + 128]
                    rhs = R[:, f * 1024 + nn2 * 512: f * 1024 + (nn2 + 1) * 512]
                    fns.append(lambda e, o=eo[:, :], l=lhs, r=rhs, st=(f == 0):
                               e.matmul(o, l, r, start=st, stop=False))
            for nn2 in (0, 1):
                wrec = p.reg.get(f"w2p{f}.{nn2}", {}).get("w")
                assert wrec is not None and wrec[0] in ("act", "dve")
            p.stage("pe", fns,
                    reads=[f"utc{uh}", f"w2p{f}.0", f"w2p{f}.1"],
                    writes=list(eoregs))

        for f in range(32):
            if g == 0 and w2r_queue:
                w2r_gp.dma(*w2_piece(*w2r_queue.pop(0)))
                w2r_sc.dma(*w2_piece(*w2r_queue.pop(0)))
            w2r_gp.flush(1)
            w2r_sc.flush(1)
            emit_w1(f)
            if f >= 1:
                emit_w2(f - 1)
            if f >= 6 and f % 6 == 0 and (f - 6) // 6 < len(epi_chunks):
                epi_chunks[(f - 6) // 6]()
        emit_w2(31)
        if g == 0:
            p.stage("sync", [lambda e: e.dma_start(BR1, d_brep[:, 0:512])],
                    writes=["stg2b0", "stg2b1", "brep"], inc_sem="sm")
            p.stage("sync", [lambda e: e.dma_start(BR2, d_brep[:, 512:768])],
                    writes=["stg2b1", "brep"], inc_sem="sm")
            p.stage("sync", [lambda e: e.dma_start(BR3, d_brep[:, 768:1024])],
                    writes=["pk", "v65", "brep"], inc_sem="sm")
            p.stage("sync", [lambda e: e.dma_start(B2ROW[0:1, :], d_b2[:, :])],
                    writes=["pk", "b2row"], inc_sem="sm")
        fns = []
        for s2 in range(2):
            for nn2 in range(2):
                eo = PB[eob[s2][nn2]]
                fns.append(lambda e, o=eo[:, :], r=B2ROW[0:1, nn2 * 512:(nn2 + 1) * 512]:
                           e.matmul(o, ONES[0:1, 0:128], r, start=False, stop=True))
        p.stage("pe", fns, reads=["ones", "b2row"], writes=list(eoregs))

    def bc_epi_y(g, sub_info):
        s1e = (2, 3) if g % 2 == 0 else (4, 5)
        for s2 in range(2):
            si = sub_info[s2]
            xa, xreg = si["xa"], si["xreg"]
            ybuf = WSTG if s2 == 0 else T1
            yregs = ["wstg0", "wstg1"] if s2 == 0 else ["t1.0", "t1.1"]
            banks = (0, 1) if s2 == 0 else s1e
            p.stage("dve", [lambda e, o=ybuf[:, :], a=xa, r=si["rstd2"],
                            m=si["mean2"]:
                            e.tensor_scalar(o, a, m, r, ALU.subtract, ALU.mult)],
                    reads=[xreg, si["rg2"]], writes=list(yregs))
            for nn2 in range(2):
                eo = PB[banks[nn2]]
                p.stage("dve", [lambda e, o=ybuf[:, nn2 * 512:(nn2 + 1) * 512],
                                a=ybuf[:, nn2 * 512:(nn2 + 1) * 512], b=eo[:, :]:
                                e.tensor_tensor(o, a, b, ALU.add)],
                        reads=[f"pb{banks[nn2]}", yregs[nn2]], writes=[yregs[nn2]])

    def bc_epi_rest_chunks(g, sub_info):
        chunks = []
        for s2 in range(2):
            si = sub_info[s2]
            xa, xreg, gi, tok0 = si["xa"], si["xreg"], si["gi"], si["tok0"]
            ybuf = WSTG if s2 == 0 else T1
            yregs = ["wstg0", "wstg1"] if s2 == 0 else ["t1.0", "t1.1"]

            def phase0(xa=xa, xreg=xreg, ybuf=ybuf, yregs=yregs):
                mean, rstd, rg = ln_stats(ybuf[:, :], yregs)
                p.stage("dve", [lambda e, o=ybuf[:, :], a=ybuf[:, :], r=rstd, m=mean:
                                e.tensor_scalar(o, a, m, r, ALU.subtract, ALU.mult)],
                        reads=yregs + [rg], writes=list(yregs))
                p.stage("dve", [lambda e, o=ybuf[:, :], a=ybuf[:, :]:
                                e.tensor_mul(o, a, GREP)],
                        reads=yregs + grep_regions, writes=list(yregs))
                p.stage("dve", [lambda e, o=ybuf[:, 0:512], a=ybuf[:, 0:512]:
                                e.tensor_add(o, a, BR1),
                                lambda e, o=ybuf[:, 512:768], a=ybuf[:, 512:768]:
                                e.tensor_add(o, a, BR2),
                                lambda e, o=ybuf[:, 768:1024], a=ybuf[:, 768:1024]:
                                e.tensor_add(o, a, BR3)],
                        reads=yregs + ["brep"], writes=list(yregs))

            def phase1(xa=xa, xreg=xreg, gi=gi, tok0=tok0, ybuf=ybuf, yregs=yregs):
                mean3, rstd3, rg3 = ln_stats(ybuf[:, :], yregs)
                p.stage("dve", [lambda e, o=ybuf[:, :], a=ybuf[:, :], r=rstd3, m=mean3:
                                e.tensor_scalar(o, a, m, r, ALU.subtract, ALU.mult)],
                        reads=yregs + [rg3], writes=list(yregs))
                p.stage("dve", [lambda e, o=xa, a=xa, b=ybuf[:, :]:
                                e.tensor_add(o, a, b)],
                        reads=yregs + [xreg], writes=[xreg])
                p.stage("sync", [lambda e, o=d_out[tok0:tok0 + 128, :], ii=xa:
                                 e.dma_start(o, ii)],
                        reads=[xreg], inc_sem=f"o{gi % 4}")
                nxt = gi + 4
                if nxt < 2 * NSUB:
                    a = nxt - NSUB
                    dst = XT[:, (nxt % 4) * D:((nxt % 4) + 1) * D]
                    p.stage("sync", [lambda e, o=dst, aa=a:
                                     e.dma_start(o, d_x[aa * 128:(aa + 1) * 128, :])],
                            writes=[tx_reg[nxt % 4]], inc_sem=f"x{nxt % 4}")

            chunks.append(phase0)
            chunks.append(phase1)
        return chunks

    infos = bc_front(0) if bc_grps >= 1 else None
    prev_infos = None
    for g in range(max(bc_grps, 0)):
        chunks = bc_epi_rest_chunks(g - 1, prev_infos) if prev_infos is not None else ()
        bc_ffn(g, chunks)
        bc_epi_y(g, infos)
        nxt_infos = bc_front(g + 1) if g + 1 < bc_grps else None
        prev_infos = infos
        infos = nxt_infos
    if prev_infos is not None:
        for ch in bc_epi_rest_chunks(bc_grps - 1, prev_infos):
            ch()

    w2r_gp.flush()
    w2r_sc.flush()
    for osl in range(4):
        if p.cnt[f"o{osl}"]:
            p.raw_wait("sync", (f"o{osl}", p.cnt[f"o{osl}"]))
    if trunc is not None:
        for sname in ("pe", "act", "dve"):
            if p.cnt[sname]:
                p.raw_wait("sync", (sname, p.cnt[sname]))

    with stack:
        sems = {}
        for sname in sorted(p.cnt.keys()):
            sems[sname] = stack.enter_context(nc.semaphore(sname))
        block = stack.enter_context(nc.Block())

        def emitter(key):
            def run(eng):
                for (waits, fn, inc) in p.items[key]:
                    for (s, v) in waits:
                        eng.wait_ge(sems[s], v)
                    if fn is None:
                        continue
                    ins = fn(eng)
                    if inc is not None:
                        ins.then_inc(sems[inc[0]], inc[1])
            return run

        block.tensor(emitter("pe"))
        block.scalar(emitter("act"))
        block.vector(emitter("dve"))
        block.sync(emitter("sync"))
        block.gpsimd(emitter("gp"))

    return nc


_NC_CACHE = []


def kernel(L0_attn_x, L0_attn_W_qkv, L0_attn_b_qkv, L0_attn_W_proj, L0_attn_b_proj,
           L0_ffn_router_w, L0_ffn_router_b, L0_ffn_w1, L0_ffn_b1, L0_ffn_w2,
           L0_ffn_b2, L0_ffn_gamma, L0_ffn_beta):
    x = np.ascontiguousarray(np.asarray(L0_attn_x, dtype=np.float32))
    wqkv = np.ascontiguousarray(np.asarray(L0_attn_W_qkv, dtype=np.float32))
    bqkv = np.asarray(L0_attn_b_qkv, dtype=np.float32)
    w1 = np.ascontiguousarray(np.asarray(L0_ffn_w1, dtype=np.float32)[0])
    b1 = np.asarray(L0_ffn_b1, dtype=np.float32)[0]
    w2 = np.ascontiguousarray(np.asarray(L0_ffn_w2, dtype=np.float32)[0])
    b2 = np.asarray(L0_ffn_b2, dtype=np.float32)[0]
    gamma = np.asarray(L0_ffn_gamma, dtype=np.float32)
    beta = np.asarray(L0_ffn_beta, dtype=np.float32)

    bkv = np.ascontiguousarray(bqkv[D:3 * D][None, :].astype(BFnp))
    bqc = np.ascontiguousarray(bqkv[0:D].reshape(8, 128).T)
    b1c = np.ascontiguousarray(b1.reshape(32, 128).T)
    b2row = np.ascontiguousarray(b2[None, :].astype(BFnp))
    grep = np.ascontiguousarray(np.broadcast_to(gamma[None, :], (128, D)))
    brep = np.ascontiguousarray(np.broadcast_to(beta[None, :], (128, D)))
    ident = np.eye(128, dtype=BFnp)
    consts = np.ascontiguousarray(
        np.tile(np.array([[EPS_LN, 0.0, 1.0, EPS_ATTN]], dtype=np.float32), (128, 1)))

    if not _NC_CACHE:
        _NC_CACHE.append(build_program())
    nc = _NC_CACHE[0]
    in_maps = []
    for c in range(8):
        in_maps.append({
            "x": np.ascontiguousarray(x[c]),
            "wqkv": wqkv, "bkv": bkv, "bqc": bqc,
            "w1": w1, "b1c": b1c, "w2": w2, "b2row": b2row,
            "grep": grep, "brep": brep, "ident": ident, "consts": consts,
        })
    res = run_bass_kernel_spmd(nc, in_maps, core_ids=list(range(8)))
    out = np.stack([np.asarray(res.results[c]["out"]) for c in range(8)], axis=0)
    return out.astype(np.float32)


# version 31
# speedup vs baseline: 1.0974x; 1.0069x over previous
import numpy as np
import ml_dtypes
from collections import defaultdict
from contextlib import ExitStack

import concourse.bass as bass
import concourse.mybir as mybir
from concourse.bass_utils import run_bass_kernel_spmd

F32 = mybir.dt.float32
BF16 = mybir.dt.bfloat16
AF = mybir.ActivationFunctionType
ALU = mybir.AluOpType
BFnp = ml_dtypes.bfloat16

S = 2048
D = 1024
H = 16
FF = 4096
NSUB = S // 128
NGRP = 8
EPS_LN = 1e-5
EPS_ATTN = 1e-6

ENGS = ("pe", "act", "dve", "sync", "gp")
ESEM = {"pe": "pe", "act": "act", "dve": "dve", "sync": "sm", "gp": "gb0"}
COMPUTE_SEMS = ("pe", "act", "dve", "gpc")


class Plan:

    def __init__(self):
        self.items = {e: [] for e in ENGS}
        self.cnt = defaultdict(int)
        self.lastw = {e: {} for e in ENGS}
        self.reg = {}
        self.nstages = 0
        self.limit = None

    def _emit(self, eng, fn, waits, inc, inc_sem):
        merged = {}
        for t in waits:
            if t is None:
                continue
            s, v = t
            if v > merged.get(s, -1):
                merged[s] = v
        wl = []
        for s, v in merged.items():
            if v > self.lastw[eng].get(s, -1):
                wl.append((s, v))
                self.lastw[eng][s] = v
        tok = None
        rec = None
        if inc:
            sem = inc_sem or ESEM[eng]
            amt = 1 if sem in COMPUTE_SEMS else 16
            self.cnt[sem] += amt
            tok = (sem, self.cnt[sem])
            rec = (sem, amt)
        self.items[eng].append((wl, fn, rec))
        return tok

    def stage(self, eng, fns, reads=(), writes=(), extra=(), inc_sem=None):
        self.nstages += 1
        if self.limit is not None and self.nstages > self.limit:
            return ("sm", 0)
        waits = list(extra)
        for r in reads:
            rec = self.reg.get(r)
            if rec and rec["w"] is not None:
                waits.append(rec["w"])
        for wr in writes:
            rec = self.reg.get(wr)
            if rec:
                if rec["w"] is not None:
                    waits.append(rec["w"])
                waits.extend(rec["r"])
        tok = None
        n = len(fns)
        for k, fn in enumerate(fns):
            tok = self._emit(eng, fn, waits if k == 0 else (),
                             inc=(k == n - 1), inc_sem=inc_sem)
        for r in reads:
            self.reg.setdefault(r, {"w": None, "r": []})["r"].append(tok)
        for wr in writes:
            self.reg[wr] = {"w": tok, "r": []}
        return tok

    def set_writer(self, region, tok):
        self.reg[region] = {"w": tok, "r": []}

    def raw_wait(self, eng, tok):
        self.items[eng].append(([tok], None, None))


def build_program(trunc=None, stage_limit=None):

    nc = bass.Bass("TRN2", target_bir_lowering=False, debug=False)

    d_x = nc.dram_tensor("x", [S, D], F32, kind="ExternalInput")
    d_wqkv = nc.dram_tensor("wqkv", [D, 3 * D], F32, kind="ExternalInput")
    d_bkv = nc.dram_tensor("bkv", [1, 2 * D], BF16, kind="ExternalInput")
    d_bqc = nc.dram_tensor("bqc", [128, 8], F32, kind="ExternalInput")
    d_w1 = nc.dram_tensor("w1", [D, FF], F32, kind="ExternalInput")
    d_b1c = nc.dram_tensor("b1c", [128, 32], F32, kind="ExternalInput")
    d_w2 = nc.dram_tensor("w2", [FF, D], F32, kind="ExternalInput")
    d_b2 = nc.dram_tensor("b2row", [1, D], BF16, kind="ExternalInput")
    d_grep = nc.dram_tensor("grep", [128, D], F32, kind="ExternalInput")
    d_brep = nc.dram_tensor("brep", [128, D], F32, kind="ExternalInput")
    d_id = nc.dram_tensor("ident", [128, 128], BF16, kind="ExternalInput")
    d_const = nc.dram_tensor("consts", [128, 4], F32, kind="ExternalInput")
    d_out = nc.dram_tensor("out", [S, D], F32, kind="ExternalOutput")

    p = Plan()
    p.limit = stage_limit
    stack = ExitStack()
    sb = lambda n, shp, dt: stack.enter_context(nc.sbuf_tensor(n, shp, dt))
    psa = lambda n, shp, dt: stack.enter_context(nc.psum_tensor(n, shp, dt))

    R = sb("R", [128, 32768], BF16)
    WQOFF = 8192
    W1 = sb("W1", [128, 8 * FF], BF16)
    PQT = sb("PQT", [128, 8 * S], BF16)
    H1T = sb("H1T", [128, 8 * 256], BF16)
    XT = sb("XT", [128, 4 * D], F32)
    PK = sb("PK", [128, D], BF16)
    V65 = sb("V65", [128, H * 65], BF16)
    ATTN = sb("ATTN", [128, D], F32)
    H1B = ATTN[:, 0:512].bitcast(BF16)
    STG2 = sb("STG2", [128, D], F32)
    T1 = sb("T1", [128, D], F32)
    WSTG = sb("WSTG", [128, D], F32)
    KVB = sb("KVB", [128, 8 * 130], BF16)
    KVSB = sb("KVSB", [128, 8 * 130], F32)
    GREP = KVSB[:, 0:1024]
    BQC = sb("BQC", [128, 8], F32)
    B1C = sb("B1C", [128, 32], F32)
    IDENT = sb("IDENT", [128, 128], BF16)
    ONES = sb("ONES", [1, 128], BF16)
    SCR = sb("SCR", [128, 14 * 12], F32)
    CONSTS = sb("CONSTS", [128, 4], F32)
    DEN = sb("DEN", [128, H], F32)
    UTC = STG2[:, 0:256].bitcast(BF16)
    BR1 = STG2[:, 256:768]
    BR2 = STG2[:, 768:1024]
    BR3 = V65[:, 0:1024].bitcast(F32)[:, 0:256]
    B2ROW = PK[:, :]
    BKV = R[0:1, 6144:8192]

    nc.const_aps.register(F32, EPS_LN, CONSTS[:, 0:1])
    nc.const_aps.register(F32, 0.0, CONSTS[:, 1:2])
    nc.const_aps.register(F32, 1.0, CONSTS[:, 2:3])
    nc.const_aps.register(F32, EPS_ATTN, CONSTS[:, 3:4])

    PB = [psa(f"pb{i}", [128, 512], F32) for i in range(6)]
    TRAB = [psa("tra", [128, 512], F32), psa("trb", [128, 512], F32)]
    TRBF = [t[:, :].bitcast(BF16) for t in TRAB]
    TRREG = ["tra", "trb"]
    A_PROJ = [0, 1, 5, 2]
    KVBANKS = [2, 3, 4]
    EOB = [[0, 1], [2, 3]]

    def kv_pair_ap(pr):
        return PB[KVBANKS[pr // 3]][:, (pr % 3) * 130:((pr % 3) + 1) * 130]

    def kv_bank(pr):
        return f"pb{KVBANKS[pr // 3]}"

    ln_id = [0]

    def ln_stats(src_ap, src_regions):
        slot = ln_id[0] % 12
        ln_id[0] += 1
        off = slot * 14
        st = SCR[:, off:off + 12]
        mv = SCR[:, off + 12:off + 14]
        rstd = mv[:, 1:2]
        rg = f"scr{slot}"
        p.stage("dve", [lambda e, a=st, b=src_ap: e.bn_stats(a[:, 0:6], b[:, 0:512]),
                        lambda e, a=st, b=src_ap: e.bn_stats(a[:, 6:12], b[:, 512:1024])],
                reads=list(src_regions), writes=[rg])
        p.stage("dve", [lambda e, a=mv, b=st: e.bn_aggr(a, b)], reads=[rg], writes=[rg])
        p.stage("act", [lambda e, a=rstd, b=mv: e.activation(a, b[:, 1:2], AF.Sqrt,
                                                           bias=EPS_LN, scale=1.0)],
                reads=[rg, "consts"], writes=[rg])
        p.stage("dve", [lambda e, a=rstd: e.reciprocal(a, a)], reads=[rg], writes=[rg])
        return mv[:, 0:1], rstd, rg

    class Streamer:

        def __init__(self, qe, bufs, buf_regions, buf_sems, cast_engs):
            self.qe = qe
            self.bufs = bufs
            self.bregs = buf_regions
            self.bsems = buf_sems
            self.cast_engs = cast_engs
            self.pending = []
            self.ci = 0
            self.bi = 0

        def dma(self, dram_ap, dest_ap, dest_regions, extra=()):
            b = self.bi % len(self.bufs)
            self.bi += 1
            while any(pb == b for (pb, _, _) in self.pending):
                self.flush(1)
            p.stage(self.qe, [lambda e, o=self.bufs[b], i=dram_ap: e.dma_start(o, i)],
                    writes=[self.bregs[b]], extra=list(extra), inc_sem=self.bsems[b])
            self.pending.append((b, dest_ap, dest_regions))

        def flush(self, n=10 ** 9):
            done = 0
            while self.pending and done < n:
                b, dest, dregs = self.pending.pop(0)
                eng = self.cast_engs[self.ci % len(self.cast_engs)]
                self.ci += 1
                if eng == "act":
                    fn = lambda e, o=dest, i=self.bufs[b]: e.activation(o, i, AF.Copy)
                else:
                    fn = lambda e, o=dest, i=self.bufs[b]: e.tensor_copy(o, i)
                p.stage(eng, [fn], reads=[self.bregs[b]], writes=list(dregs))
                done += 1

    wstg_bufs = [WSTG[:, i * 512:(i + 1) * 512] for i in range(2)]
    stg2_bufs = [STG2[:, i * 512:(i + 1) * 512] for i in range(2)]
    gp_st = Streamer("gp", wstg_bufs, ["wstg0", "wstg1"], ["gb0", "gb1"], ["act", "dve"])
    sc_st = Streamer("sync", stg2_bufs, ["stg2b0", "stg2b1"], ["sb0", "sb1"], ["dve", "act"])

    def wq_piece(k, c2):
        return (d_wqkv[k * 128:(k + 1) * 128, c2 * 512:(c2 + 1) * 512],
                R[:, WQOFF + k * 3072 + c2 * 512: WQOFF + k * 3072 + (c2 + 1) * 512],
                [f"wq{k}.{c2}"])

    def w1_piece(k, j):
        return (d_w1[k * 128:(k + 1) * 128, j * 512:(j + 1) * 512],
                W1[:, k * FF + j * 512: k * FF + (j + 1) * 512], [f"w1p{k}.{j}"])

    def w2_piece(f, nn):
        regs = [f"w2p{f}.{nn}"]
        addr = f * 1024 + nn * 512 - WQOFF
        if addr >= 0:
            k = addr // 3072
            c2 = (addr % 3072) // 512
            regs.append(f"wq{k}.{c2}")
        return (d_w2[f * 128:(f + 1) * 128, nn * 512:(nn + 1) * 512],
                R[:, f * 1024 + nn * 512: f * 1024 + (nn + 1) * 512], regs)

    smalls = [
        (IDENT[:, :], d_id[:, :], "ident"),
        (CONSTS[:, :], d_const[:, :], "consts"),
        (BKV, d_bkv[:, :], "bkv"),
        (BQC[:, :], d_bqc[:, :], "bqc"),
        (B1C[:, :], d_b1c[:, :], "b1c"),
    ]
    for (dst, srcd, rg) in smalls:
        p.stage("sync", [lambda e, o=dst, i=srcd: e.dma_start(o, i)],
                writes=[rg], inc_sem="sm")
    sm_all = ("sm", p.cnt["sm"])
    for (_, _, rg) in smalls:
        p.set_writer(rg, sm_all)
    bkv_regions = [f"w2p{fk}.{nn}" for fk in (6, 7) for nn in (0, 1)]
    for rg in bkv_regions:
        p.set_writer(rg, sm_all)

    p.stage("dve", [lambda e: e.memset(ONES[:, :], 1.0)], writes=["ones"])
    p.stage("dve", [lambda e: e.memset(V65[:, :], 1.0)], writes=["v65"])
    p.stage("dve", [lambda e: e.memset(KVB[:, :], 0.0)], writes=["kvb"])

    tx_reg = ["xt0", "xt1", "xt2", "xt3"]
    for sl in range(4):
        p.stage("sync", [lambda e, s=sl: e.dma_start(
            XT[:, s * D:(s + 1) * D], d_x[s * 128:(s + 1) * 128, :])],
                writes=[tx_reg[sl]], inc_sem=f"x{sl}")

    kv_pieces = [(k, c2) for c2 in (2, 3, 4, 5) for k in range(8)]
    q_pieces = [(k, c2) for c2 in (0, 1) for k in range(8)]
    w1_pieces = [(k, j) for j in range(8) for k in range(8)]
    w2e_pieces = [(f, n) for f in range(6) for n in range(2)]
    w2r_pieces = [(f, n) for f in range(6, 32) for n in range(2)]

    nbatches = 4 if trunc is None or trunc[0] != "P" else trunc[1]
    for r in range(nbatches):
        for t in range(4):
            gp_st.dma(*wq_piece(*kv_pieces[r * 8 + t]))
            sc_st.dma(*wq_piece(*kv_pieces[r * 8 + 4 + t]))
        gp_st.flush()
        sc_st.flush()

    proj_i = [0]
    a_subs = NSUB if trunc is None or trunc[0] != "A" else trunc[1]
    bc_grps = NGRP if trunc is None or trunc[0] != "BC" else trunc[1]
    if trunc is not None and trunc[0] in ("A", "P"):
        bc_grps = -1
        if trunc[0] == "P":
            a_subs = 0
    for i in range(a_subs):
        half = i % 2
        slot = i % 4
        xa = XT[:, slot * D:(slot + 1) * D]
        xreg = tx_reg[slot]

        gp_st.flush()
        sc_st.flush()
        if i < 2:
            for t in range(4):
                gp_st.dma(*wq_piece(*q_pieces[i * 8 + t]))
                sc_st.dma(*wq_piece(*q_pieces[i * 8 + 4 + t]))
        elif i < 13:
            for t in range(3):
                idx = (i - 2) * 6 + t * 2
                if idx < 64:
                    gp_st.dma(*w1_piece(*w1_pieces[idx]))
                if idx + 1 < 64:
                    sc_st.dma(*w1_piece(*w1_pieces[idx + 1]))
        if 8 <= i < 14:
            gp_st.dma(*w2_piece(*w2e_pieces[(i - 8) * 2]))
            sc_st.dma(*w2_piece(*w2e_pieces[(i - 8) * 2 + 1]))

        mean, rstd, rg = ln_stats(xa, [xreg])
        p.stage("dve", [lambda e, o=H1B[:, :], a=xa, r=rstd, m=mean:
                        e.tensor_scalar(o, a, m, r, ALU.subtract, ALU.mult)],
                reads=[xreg, rg], writes=["h1b", "attn.0"])

        for hh in range(2):
            fns = []
            for b in range(4):
                c = hh * 4 + b
                fns.append(lambda e, o=TRBF[hh][:, b * 128:(b + 1) * 128],
                           ii=H1B[:, c * 128:(c + 1) * 128]: e.transpose(o, ii, IDENT[:, :]))
            p.stage("pe", fns, reads=["h1b", "ident"], writes=[TRREG[hh]])
            out3d = H1T[:, :].rearrange("p (c t) -> p c t", t=256)[:, hh * 4:hh * 4 + 4,
                                                                  half * 128:(half + 1) * 128]
            in3d = TRBF[hh][:, 0:512].rearrange("p (c t) -> p c t", t=128)
            p.stage("act", [lambda e, o=out3d, ii=in3d: e.copy(o, ii)],
                    reads=[TRREG[hh]], writes=["h1t"])

        for j in range(4):
            bidx = A_PROJ[proj_i[0] % 4]
            proj_i[0] += 1
            bank = PB[bidx]
            c2 = 2 + j
            fns = []
            for k in range(8):
                lhs = H1T[:, k * 256 + half * 128: k * 256 + (half + 1) * 128]
                rhs = R[:, WQOFF + k * 3072 + c2 * 512: WQOFF + k * 3072 + (c2 + 1) * 512]
                fns.append(lambda e, o=bank[:, :], l=lhs, r=rhs, st=(k == 0):
                           e.matmul(o, l, r, start=st, stop=False))
            fns.append(lambda e, o=bank[:, :], r=BKV[0:1, j * 512:(j + 1) * 512]:
                       e.matmul(o, ONES[0:1, 0:128], r, start=False, stop=True))
            p.stage("pe", fns,
                    reads=["h1t", "ones"] + bkv_regions + [f"wq{k}.{c2}" for k in range(8)],
                    writes=[f"pb{bidx}"])
            if j < 2:
                tb = T1[:, (j % 2) * 512:(j % 2) * 512 + 512]
                treg = f"t1.{j % 2}"
                tw = [treg]
                p.stage("dve", [lambda e, o=tb, ii=bank[:, :]:
                                e.tensor_single_scalar(o, ii, 0.0, ALU.min)],
                        reads=[f"pb{bidx}"], writes=tw)
                p.stage("act", [lambda e, o=tb, ii=tb: e.activation(o, ii, AF.Exp)],
                        reads=[treg], writes=tw)
                p.stage("dve", [lambda e, o=PK[:, j * 512:(j + 1) * 512], ii=bank[:, :], t=tb:
                                e.scalar_tensor_tensor(o, ii, 0.0, t, ALU.max, ALU.add)],
                        reads=[f"pb{bidx}", treg], writes=["pk"])
            else:
                hb = (j - 2) * 8
                out_v = V65[:, :].rearrange("p (h m) -> p h m", m=65)[:, hb:hb + 8, 0:64]
                in_v = bank[:, :].rearrange("p (h m) -> p h m", m=64)
                p.stage("dve", [lambda e, o=out_v, ii=in_v: e.tensor_copy(o, ii)],
                        reads=[f"pb{bidx}"], writes=["v65"])

        for half8 in range(2):
            fns = [lambda e, o=kv_pair_ap(pr), l=PK[:, pr * 128:(pr + 1) * 128],
                   r=V65[:, pr * 130:(pr + 1) * 130]:
                   e.matmul(o, l, r, start=True, stop=True)
                   for pr in range(half8 * 4, half8 * 4 + 4)]
            p.stage("pe", fns, reads=["pk", "v65"],
                    writes=sorted(set(kv_bank(pr) for pr in range(half8 * 4, half8 * 4 + 4))))
        for bi, (b0, wid) in enumerate(((0, 390), (390, 390), (780, 260))):
            bank = PB[KVBANKS[bi]]
            if i == 0:
                p.stage("dve", [lambda e, o=KVSB[:, b0:b0 + wid], ii=bank[:, 0:wid]:
                                e.tensor_copy(o, ii)],
                        reads=[f"pb{KVBANKS[bi]}"], writes=[f"kvsb{bi}"])
            else:
                p.stage("dve", [lambda e, o=KVSB[:, b0:b0 + wid], ii=bank[:, 0:wid]:
                                e.tensor_tensor(o, o, ii, ALU.add)],
                        reads=[f"pb{KVBANKS[bi]}", f"kvsb{bi}"], writes=[f"kvsb{bi}"])

        nxt = i + 4
        if nxt < 2 * NSUB:
            a = nxt if nxt < NSUB else nxt - NSUB
            dst = XT[:, (nxt % 4) * D:((nxt % 4) + 1) * D]
            p.stage("sync", [lambda e, o=dst, aa=a:
                             e.dma_start(o, d_x[aa * 128:(aa + 1) * 128, :])],
                    writes=[tx_reg[nxt % 4]], inc_sem=f"x{nxt % 4}")

        if i % 2 == 1:
            tile = i // 2
            gp_st.flush()
            sc_st.flush()
            for c in range(8):
                bidx = A_PROJ[proj_i[0] % 4]
                proj_i[0] += 1
                bank = PB[bidx]
                fns = []
                for k in range(8):
                    lhs = R[:, WQOFF + k * 3072 + c * 128: WQOFF + k * 3072 + (c + 1) * 128]
                    rhs = H1T[:, k * 256:(k + 1) * 256]
                    fns.append(lambda e, o=bank[:, 0:256], l=lhs, r=rhs,
                               st=(k == 0), sp=(k == 7): e.matmul(o, l, r, start=st, stop=sp))
                p.stage("pe", fns,
                        reads=["h1t"] + [f"wq{k}.{(c * 128) // 512}" for k in range(8)],
                        writes=[f"pb{bidx}"])
                ta = T1[:, (c % 2) * 512:(c % 2) * 512 + 256]
                tc = T1[:, (c % 2) * 512 + 256:(c % 2) * 512 + 512]
                tb2 = ATTN[:, (c % 2) * 512:(c % 2) * 512 + 256]
                treg, areg = f"t1.{c % 2}", f"attn.{c % 2}"
                p.stage("dve", [lambda e, o=ta, ii=bank[:, 0:256], bq=BQC[:, c:c + 1]:
                                e.tensor_scalar(o, ii, bq, 0.0, ALU.add, ALU.min),
                                lambda e, o=tc, ii=bank[:, 0:256], bq=BQC[:, c:c + 1]:
                                e.tensor_scalar(o, ii, bq, 0.0, ALU.add, ALU.max)],
                        reads=[f"pb{bidx}", "bqc"], writes=[treg])
                p.stage("act", [lambda e, o=tb2, ii=ta: e.activation(o, ii, AF.Exp)],
                        reads=[treg],
                        writes=[areg] + (["h1b"] if c % 2 == 0 else []))
                dst = PQT[:, c * S + tile * 256: c * S + tile * 256 + 256]
                p.stage("dve", [lambda e, o=dst, a=tc, b=tb2: e.tensor_add(o, a, b)],
                        reads=[treg, areg], writes=["pqt"])

    for pr in range(8 if a_subs == NSUB else 0):
        p.stage("dve", [lambda e, o=KVB[0:64, pr * 130: pr * 130 + 65],
                        ii=KVSB[0:64, pr * 130: pr * 130 + 65]: e.tensor_copy(o, ii)],
                reads=[f"kvsb{pr // 3}"], writes=["kvb"])
        p.stage("dve", [lambda e, o=KVB[64:128, pr * 130 + 65: pr * 130 + 130],
                        ii=KVSB[64:128, pr * 130 + 65: pr * 130 + 130]: e.tensor_copy(o, ii)],
                reads=[f"kvsb{pr // 3}"], writes=["kvb"])

    grep_regions = [f"kvsb{pr2}" for pr2 in range(3)]
    if bc_grps >= 0:
        p.stage("sync", [lambda e: e.dma_start(GREP, d_grep[:, :])],
                writes=grep_regions, inc_sem="sm2")

    gp_st.flush()
    sc_st.flush()

    pkv_bufs = [PK[:, :].bitcast(F32), V65[:, 0:1024].bitcast(F32)]
    w2r_gp = Streamer("gp", wstg_bufs, ["wstg0", "wstg1"], ["gb0", "gb1"], ["dve", "act"])
    w2r_gp.bi = gp_st.bi
    w2r_sc = Streamer("sync", pkv_bufs, ["pk", "v65"], ["kb0", "kb1"], ["act", "dve"])
    w2r_queue = list(w2r_pieces)
    if bc_grps >= 1:
        for _ in range(2):
            w2r_gp.dma(*w2_piece(*w2r_queue.pop(0)))
            w2r_sc.dma(*w2_piece(*w2r_queue.pop(0)))

    def bc_front(g):
        infos = []
        for s2 in range(2):
            gi = NSUB + g * 2 + s2
            tok0 = g * 256 + s2 * 128
            xa = XT[:, (gi % 4) * D:((gi % 4) + 1) * D]
            xreg = tx_reg[gi % 4]

            s1e = (2, 3) if g % 2 == 0 else (4, 5)
            at_banks = [(TRAB[0], "tra", (0, 1, 2)), (TRAB[1], "trb", (3, 4, 5)),
                        (PB[s1e[0]], f"pb{s1e[0]}", (6,)),
                        (PB[s1e[1]], f"pb{s1e[1]}", (7,))]
            for (bank, breg, prs) in at_banks:
                fns = [lambda e, o=bank[:, qq * 130:(qq + 1) * 130],
                       l=PQT[:, pr * S + tok0: pr * S + tok0 + 128],
                       r=KVB[:, pr * 130:(pr + 1) * 130]:
                       e.matmul(o, l, r, start=True, stop=True)
                       for qq, pr in enumerate(prs)]
                p.stage("pe", fns, reads=["pqt", "kvb"], writes=[breg])
            fns = []
            for (bank, breg, prs) in at_banks:
                nds = 2 * len(prs)
                den_s = DEN[:, 2 * prs[0]: 2 * prs[0] + nds]
                in_d = bank[:, 0:len(prs) * 130].rearrange(
                    "p (x m) -> p x m", m=65)[:, :, 64:65].squeeze(2)
                fns.append(lambda e, o=den_s, ii=in_d: e.tensor_copy(o, ii))
            p.stage("dve", fns,
                    reads=["tra", "trb", f"pb{s1e[0]}", f"pb{s1e[1]}"],
                    writes=["den", "zr"])
            p.stage("dve", [lambda e: e.tensor_single_scalar(DEN[:, :], DEN[:, :],
                                                             EPS_ATTN, ALU.add)],
                    reads=["den"], writes=["den"])
            p.stage("dve", [lambda e: e.reciprocal(DEN[:, :], DEN[:, :])],
                    reads=["den"], writes=["den", "zr"])
            fns = []
            for (bank, breg, prs) in at_banks[:2]:
                for qq, pr in enumerate(prs):
                    for hh2 in range(2):
                        hd = 2 * pr + hh2
                        srcp = bank[:, qq * 130 + hh2 * 65: qq * 130 + hh2 * 65 + 64]
                        fns.append(lambda e, o=ATTN[:, hd * 64:(hd + 1) * 64],
                                   ii=srcp, z=DEN[:, hd:hd + 1]:
                                   e.activation(o, ii, AF.Copy, bias=0.0, scale=z))
            p.stage("act", fns, reads=["tra", "trb", "zr", "consts"],
                    writes=["attn.0", "attn.1", "h1b"])
            fns = []
            for (bank, breg, prs) in at_banks[2:]:
                for qq, pr in enumerate(prs):
                    for hh2 in range(2):
                        hd = 2 * pr + hh2
                        srcp = bank[:, qq * 130 + hh2 * 65: qq * 130 + hh2 * 65 + 64]
                        fns.append(lambda e, o=ATTN[:, hd * 64:(hd + 1) * 64],
                                   ii=srcp, z=DEN[:, hd:hd + 1]:
                                   e.tensor_scalar(o, ii, z, 0.0, ALU.mult, ALU.add))
            p.stage("dve", fns,
                    reads=[f"pb{s1e[0]}", f"pb{s1e[1]}", "zr"],
                    writes=["attn.0", "attn.1", "h1b"])

            mean, rstd, rg = ln_stats(ATTN[:, :], ["attn.0", "attn.1"])
            p.stage("dve", [lambda e, o=ATTN[:, :], a=ATTN[:, :], r=rstd, m=mean:
                            e.tensor_scalar(o, a, m, r, ALU.subtract, ALU.mult)],
                    reads=["attn.0", "attn.1", rg],
                    writes=["attn.0", "attn.1", "h1b"])
            p.stage("dve", [lambda e, o=xa, a=xa, b=ATTN[:, :]: e.tensor_add(o, a, b)],
                    reads=["attn.0", "attn.1", xreg], writes=[xreg])

            mean2, rstd2, rg2 = ln_stats(xa, [xreg])
            p.stage("dve", [lambda e, o=H1B[:, :], a=xa, r=rstd2, m=mean2:
                            e.tensor_scalar(o, a, m, r, ALU.subtract, ALU.mult)],
                    reads=[xreg, rg2], writes=["h1b", "attn.0"])

            for hh in range(2):
                fns = []
                for b in range(4):
                    c = hh * 4 + b
                    fns.append(lambda e,
                               o=TRBF[hh][:, b * 128:(b + 1) * 128],
                               ii=H1B[:, c * 128:(c + 1) * 128]:
                               e.transpose(o, ii, IDENT[:, :]))
                p.stage("pe", fns, reads=["h1b", "ident"], writes=[TRREG[hh]])
                out3d = H1T[:, :].rearrange("p (c t) -> p c t", t=256)[:, hh * 4:hh * 4 + 4,
                                                                      s2 * 128:(s2 + 1) * 128]
                in3d = TRBF[hh][:, 0:512].rearrange("p (c t) -> p c t", t=128)
                p.stage("act", [lambda e, o=out3d, ii=in3d: e.copy(o, ii)],
                        reads=[TRREG[hh]], writes=["h1t"])
            infos.append(dict(xa=xa, xreg=xreg, gi=gi, tok0=tok0,
                              rstd2=rstd2, mean2=mean2, rg2=rg2))
        return infos

    def bc_ffn(g, epi_chunks=()):
        s1e = (2, 3) if g % 2 == 0 else (4, 5)
        eob = [[0, 1], list(s1e)]

        def emit_w1(f):
            uh = f % 2
            ut = TRAB[uh][:, 0:256]
            fns = []
            for k in range(8):
                lhs = W1[:, k * FF + f * 128: k * FF + f * 128 + 128]
                rhs = H1T[:, k * 256: k * 256 + 256]
                fns.append(lambda e, o=ut, l=lhs, r=rhs, st=(k == 0), sp=(k == 7):
                           e.matmul(o, l, r, start=st, stop=sp))
            p.stage("pe", fns,
                    reads=["h1t"] + [f"w1p{k}.{(f * 128) // 512}" for k in range(8)],
                    writes=[TRREG[uh]])
            p.stage("act", [lambda e, o=UTC[:, uh * 256: uh * 256 + 256],
                            ii=ut, b=B1C[:, f:f + 1]:
                            e.activation(o, ii, AF.Relu, bias=b, scale=1.0)],
                    reads=[TRREG[uh], "b1c", "consts"], writes=[f"utc{uh}", "stg2b0"])

        eoregs = [f"pb{b}" for row in eob for b in row]

        def emit_w2(f):
            uh = f % 2
            fns = []
            for s2 in range(2):
                for nn2 in range(2):
                    eo = PB[eob[s2][nn2]]
                    lhs = UTC[:, uh * 256 + s2 * 128: uh * 256 + s2 * 128 + 128]
                    rhs = R[:, f * 1024 + nn2 * 512: f * 1024 + (nn2 + 1) * 512]
                    fns.append(lambda e, o=eo[:, :], l=lhs, r=rhs, st=(f == 0):
                               e.matmul(o, l, r, start=st, stop=False))
            for nn2 in (0, 1):
                wrec = p.reg.get(f"w2p{f}.{nn2}", {}).get("w")
                assert wrec is not None and wrec[0] in ("act", "dve")
            p.stage("pe", fns,
                    reads=[f"utc{uh}", f"w2p{f}.0", f"w2p{f}.1"],
                    writes=list(eoregs))

        for f in range(32):
            if g == 0 and w2r_queue:
                w2r_gp.dma(*w2_piece(*w2r_queue.pop(0)))
                w2r_sc.dma(*w2_piece(*w2r_queue.pop(0)))
            w2r_gp.flush(1)
            w2r_sc.flush(1)
            emit_w1(f)
            if f >= 1:
                emit_w2(f - 1)
            if f >= 6 and f % 6 == 0 and (f - 6) // 6 < len(epi_chunks):
                epi_chunks[(f - 6) // 6]()
        emit_w2(31)
        if g == 0:
            p.stage("sync", [lambda e: e.dma_start(BR1, d_brep[:, 0:512])],
                    writes=["stg2b0", "stg2b1", "brep"], inc_sem="sm")
            p.stage("sync", [lambda e: e.dma_start(BR2, d_brep[:, 512:768])],
                    writes=["stg2b1", "brep"], inc_sem="sm")
            p.stage("sync", [lambda e: e.dma_start(BR3, d_brep[:, 768:1024])],
                    writes=["pk", "v65", "brep"], inc_sem="sm")
            p.stage("sync", [lambda e: e.dma_start(B2ROW[0:1, :], d_b2[:, :])],
                    writes=["pk", "b2row"], inc_sem="sm")
        fns = []
        for s2 in range(2):
            for nn2 in range(2):
                eo = PB[eob[s2][nn2]]
                fns.append(lambda e, o=eo[:, :], r=B2ROW[0:1, nn2 * 512:(nn2 + 1) * 512]:
                           e.matmul(o, ONES[0:1, 0:128], r, start=False, stop=True))
        p.stage("pe", fns, reads=["ones", "b2row"], writes=list(eoregs))

    def bc_epi_y(g, sub_info):
        s1e = (2, 3) if g % 2 == 0 else (4, 5)
        for s2 in range(2):
            si = sub_info[s2]
            xa, xreg = si["xa"], si["xreg"]
            ybuf = WSTG if s2 == 0 else T1
            yregs = ["wstg0", "wstg1"] if s2 == 0 else ["t1.0", "t1.1"]
            banks = (0, 1) if s2 == 0 else s1e
            p.stage("dve", [lambda e, o=ybuf[:, :], a=xa, r=si["rstd2"],
                            m=si["mean2"]:
                            e.tensor_scalar(o, a, m, r, ALU.subtract, ALU.mult)],
                    reads=[xreg, si["rg2"]], writes=list(yregs))
            for nn2 in range(2):
                eo = PB[banks[nn2]]
                p.stage("dve", [lambda e, o=ybuf[:, nn2 * 512:(nn2 + 1) * 512],
                                a=ybuf[:, nn2 * 512:(nn2 + 1) * 512], b=eo[:, :]:
                                e.tensor_tensor(o, a, b, ALU.add)],
                        reads=[f"pb{banks[nn2]}", yregs[nn2]], writes=[yregs[nn2]])

    def bc_epi_rest_chunks(g, sub_info):
        chunks = []
        for s2 in range(2):
            si = sub_info[s2]
            xa, xreg, gi, tok0 = si["xa"], si["xreg"], si["gi"], si["tok0"]
            ybuf = WSTG if s2 == 0 else T1
            yregs = ["wstg0", "wstg1"] if s2 == 0 else ["t1.0", "t1.1"]

            def phase0(xa=xa, xreg=xreg, ybuf=ybuf, yregs=yregs):
                mean, rstd, rg = ln_stats(ybuf[:, :], yregs)
                p.stage("dve", [lambda e, o=ybuf[:, :], a=ybuf[:, :], r=rstd, m=mean:
                                e.tensor_scalar(o, a, m, r, ALU.subtract, ALU.mult)],
                        reads=yregs + [rg], writes=list(yregs))
                p.stage("dve", [lambda e, o=ybuf[:, :], a=ybuf[:, :]:
                                e.tensor_mul(o, a, GREP)],
                        reads=yregs + grep_regions, writes=list(yregs))
                p.stage("dve", [lambda e, o=ybuf[:, 0:512], a=ybuf[:, 0:512]:
                                e.tensor_add(o, a, BR1),
                                lambda e, o=ybuf[:, 512:768], a=ybuf[:, 512:768]:
                                e.tensor_add(o, a, BR2),
                                lambda e, o=ybuf[:, 768:1024], a=ybuf[:, 768:1024]:
                                e.tensor_add(o, a, BR3)],
                        reads=yregs + ["brep"], writes=list(yregs))

            def phase1(xa=xa, xreg=xreg, gi=gi, tok0=tok0, ybuf=ybuf, yregs=yregs):
                mean3, rstd3, rg3 = ln_stats(ybuf[:, :], yregs)
                p.stage("dve", [lambda e, o=ybuf[:, :], a=ybuf[:, :], r=rstd3, m=mean3:
                                e.tensor_scalar(o, a, m, r, ALU.subtract, ALU.mult)],
                        reads=yregs + [rg3], writes=list(yregs))
                p.stage("dve", [lambda e, o=xa, a=xa, b=ybuf[:, :]:
                                e.tensor_add(o, a, b)],
                        reads=yregs + [xreg], writes=[xreg])
                p.stage("sync", [lambda e, o=d_out[tok0:tok0 + 128, :], ii=xa:
                                 e.dma_start(o, ii)],
                        reads=[xreg], inc_sem=f"o{gi % 4}")
                nxt = gi + 4
                if nxt < 2 * NSUB:
                    a = nxt - NSUB
                    dst = XT[:, (nxt % 4) * D:((nxt % 4) + 1) * D]
                    p.stage("sync", [lambda e, o=dst, aa=a:
                                     e.dma_start(o, d_x[aa * 128:(aa + 1) * 128, :])],
                            writes=[tx_reg[nxt % 4]], inc_sem=f"x{nxt % 4}")

            chunks.append(phase0)
            chunks.append(phase1)
        return chunks

    infos = bc_front(0) if bc_grps >= 1 else None
    prev_infos = None
    for g in range(max(bc_grps, 0)):
        chunks = bc_epi_rest_chunks(g - 1, prev_infos) if prev_infos is not None else ()
        bc_ffn(g, chunks)
        bc_epi_y(g, infos)
        nxt_infos = bc_front(g + 1) if g + 1 < bc_grps else None
        prev_infos = infos
        infos = nxt_infos
    if prev_infos is not None:
        for ch in bc_epi_rest_chunks(bc_grps - 1, prev_infos):
            ch()

    w2r_gp.flush()
    w2r_sc.flush()
    for osl in range(4):
        if p.cnt[f"o{osl}"]:
            p.raw_wait("sync", (f"o{osl}", p.cnt[f"o{osl}"]))
    if trunc is not None:
        for sname in ("pe", "act", "dve"):
            if p.cnt[sname]:
                p.raw_wait("sync", (sname, p.cnt[sname]))

    with stack:
        sems = {}
        for sname in sorted(p.cnt.keys()):
            sems[sname] = stack.enter_context(nc.semaphore(sname))
        block = stack.enter_context(nc.Block())

        def emitter(key):
            def run(eng):
                for (waits, fn, inc) in p.items[key]:
                    for (s, v) in waits:
                        eng.wait_ge(sems[s], v)
                    if fn is None:
                        continue
                    ins = fn(eng)
                    if inc is not None:
                        ins.then_inc(sems[inc[0]], inc[1])
            return run

        block.tensor(emitter("pe"))
        block.scalar(emitter("act"))
        block.vector(emitter("dve"))
        block.sync(emitter("sync"))
        block.gpsimd(emitter("gp"))

    return nc


_NC_CACHE = []


def kernel(L0_attn_x, L0_attn_W_qkv, L0_attn_b_qkv, L0_attn_W_proj, L0_attn_b_proj,
           L0_ffn_router_w, L0_ffn_router_b, L0_ffn_w1, L0_ffn_b1, L0_ffn_w2,
           L0_ffn_b2, L0_ffn_gamma, L0_ffn_beta):
    x = np.ascontiguousarray(np.asarray(L0_attn_x, dtype=np.float32))
    wqkv = np.ascontiguousarray(np.asarray(L0_attn_W_qkv, dtype=np.float32))
    bqkv = np.asarray(L0_attn_b_qkv, dtype=np.float32)
    w1 = np.ascontiguousarray(np.asarray(L0_ffn_w1, dtype=np.float32)[0])
    b1 = np.asarray(L0_ffn_b1, dtype=np.float32)[0]
    w2 = np.ascontiguousarray(np.asarray(L0_ffn_w2, dtype=np.float32)[0])
    b2 = np.asarray(L0_ffn_b2, dtype=np.float32)[0]
    gamma = np.asarray(L0_ffn_gamma, dtype=np.float32)
    beta = np.asarray(L0_ffn_beta, dtype=np.float32)

    bkv = np.ascontiguousarray(bqkv[D:3 * D][None, :].astype(BFnp))
    bqc = np.ascontiguousarray(bqkv[0:D].reshape(8, 128).T)
    b1c = np.ascontiguousarray(b1.reshape(32, 128).T)
    b2row = np.ascontiguousarray(b2[None, :].astype(BFnp))
    grep = np.ascontiguousarray(np.broadcast_to(gamma[None, :], (128, D)))
    brep = np.ascontiguousarray(np.broadcast_to(beta[None, :], (128, D)))
    ident = np.eye(128, dtype=BFnp)
    consts = np.ascontiguousarray(
        np.tile(np.array([[EPS_LN, 0.0, 1.0, EPS_ATTN]], dtype=np.float32), (128, 1)))

    if not _NC_CACHE:
        _NC_CACHE.append(build_program())
    nc = _NC_CACHE[0]
    in_maps = []
    for c in range(8):
        in_maps.append({
            "x": np.ascontiguousarray(x[c]),
            "wqkv": wqkv, "bkv": bkv, "bqc": bqc,
            "w1": w1, "b1c": b1c, "w2": w2, "b2row": b2row,
            "grep": grep, "brep": brep, "ident": ident, "consts": consts,
        })
    res = run_bass_kernel_spmd(nc, in_maps, core_ids=list(range(8)))
    out = np.stack([np.asarray(res.results[c]["out"]) for c in range(8)], axis=0)
    return out.astype(np.float32)


# version 32
# speedup vs baseline: 1.1111x; 1.0125x over previous
import numpy as np
import ml_dtypes
from collections import defaultdict
from contextlib import ExitStack

import concourse.bass as bass
import concourse.mybir as mybir
from concourse.bass_utils import run_bass_kernel_spmd

F32 = mybir.dt.float32
BF16 = mybir.dt.bfloat16
AF = mybir.ActivationFunctionType
ALU = mybir.AluOpType
BFnp = ml_dtypes.bfloat16

S = 2048
D = 1024
H = 16
FF = 4096
NSUB = S // 128
NGRP = 8
EPS_LN = 1e-5
EPS_ATTN = 1e-6

ENGS = ("pe", "act", "dve", "sync", "gp")
ESEM = {"pe": "pe", "act": "act", "dve": "dve", "sync": "sm", "gp": "gb0"}
COMPUTE_SEMS = ("pe", "act", "dve", "gpc")


class Plan:

    def __init__(self):
        self.items = {e: [] for e in ENGS}
        self.cnt = defaultdict(int)
        self.lastw = {e: {} for e in ENGS}
        self.reg = {}
        self.nstages = 0
        self.limit = None

    def _emit(self, eng, fn, waits, inc, inc_sem):
        merged = {}
        for t in waits:
            if t is None:
                continue
            s, v = t
            if v > merged.get(s, -1):
                merged[s] = v
        wl = []
        for s, v in merged.items():
            if v > self.lastw[eng].get(s, -1):
                wl.append((s, v))
                self.lastw[eng][s] = v
        tok = None
        rec = None
        if inc:
            sem = inc_sem or ESEM[eng]
            amt = 1 if sem in COMPUTE_SEMS else 16
            self.cnt[sem] += amt
            tok = (sem, self.cnt[sem])
            rec = (sem, amt)
        self.items[eng].append((wl, fn, rec))
        return tok

    def stage(self, eng, fns, reads=(), writes=(), extra=(), inc_sem=None):
        self.nstages += 1
        if self.limit is not None and self.nstages > self.limit:
            return ("sm", 0)
        waits = list(extra)
        for r in reads:
            rec = self.reg.get(r)
            if rec and rec["w"] is not None:
                waits.append(rec["w"])
        for wr in writes:
            rec = self.reg.get(wr)
            if rec:
                if rec["w"] is not None:
                    waits.append(rec["w"])
                waits.extend(rec["r"])
        tok = None
        n = len(fns)
        for k, fn in enumerate(fns):
            tok = self._emit(eng, fn, waits if k == 0 else (),
                             inc=(k == n - 1), inc_sem=inc_sem)
        for r in reads:
            self.reg.setdefault(r, {"w": None, "r": []})["r"].append(tok)
        for wr in writes:
            self.reg[wr] = {"w": tok, "r": []}
        return tok

    def set_writer(self, region, tok):
        self.reg[region] = {"w": tok, "r": []}

    def raw_wait(self, eng, tok):
        self.items[eng].append(([tok], None, None))


def build_program(trunc=None, stage_limit=None):

    nc = bass.Bass("TRN2", target_bir_lowering=False, debug=False)

    d_x = nc.dram_tensor("x", [S, D], F32, kind="ExternalInput")
    d_wqkv = nc.dram_tensor("wqkv", [D, 3 * D], F32, kind="ExternalInput")
    d_bkv = nc.dram_tensor("bkv", [1, 2 * D], BF16, kind="ExternalInput")
    d_bqc = nc.dram_tensor("bqc", [128, 8], F32, kind="ExternalInput")
    d_w1 = nc.dram_tensor("w1", [D, FF], F32, kind="ExternalInput")
    d_b1c = nc.dram_tensor("b1c", [128, 32], F32, kind="ExternalInput")
    d_w2 = nc.dram_tensor("w2", [FF, D], F32, kind="ExternalInput")
    d_b2 = nc.dram_tensor("b2row", [1, D], BF16, kind="ExternalInput")
    d_grep = nc.dram_tensor("grep", [128, D], F32, kind="ExternalInput")
    d_brep = nc.dram_tensor("brep", [128, D], F32, kind="ExternalInput")
    d_id = nc.dram_tensor("ident", [128, 128], BF16, kind="ExternalInput")
    d_const = nc.dram_tensor("consts", [128, 4], F32, kind="ExternalInput")
    d_out = nc.dram_tensor("out", [S, D], F32, kind="ExternalOutput")

    p = Plan()
    p.limit = stage_limit
    stack = ExitStack()
    sb = lambda n, shp, dt: stack.enter_context(nc.sbuf_tensor(n, shp, dt))
    psa = lambda n, shp, dt: stack.enter_context(nc.psum_tensor(n, shp, dt))

    R = sb("R", [128, 32768], BF16)
    WQOFF = 8192
    W1 = sb("W1", [128, 8 * FF], BF16)
    PQT = sb("PQT", [128, 8 * S], BF16)
    H1T = sb("H1T", [128, 8 * 256], BF16)
    XT = sb("XT", [128, 4 * D], F32)
    PK = sb("PK", [128, D], BF16)
    V65 = sb("V65", [128, H * 65], BF16)
    ATTN = sb("ATTN", [128, D], F32)
    H1B = ATTN[:, 0:512].bitcast(BF16)
    STG2 = sb("STG2", [128, D], F32)
    T1 = sb("T1", [128, D], F32)
    WSTG = sb("WSTG", [128, D], F32)
    KVB = sb("KVB", [128, 8 * 130], BF16)
    KVSB = sb("KVSB", [128, 8 * 130], F32)
    GREP = KVSB[:, 0:1024]
    BQC = sb("BQC", [128, 8], F32)
    B1C = sb("B1C", [128, 32], F32)
    IDENT = sb("IDENT", [128, 128], BF16)
    ONES = sb("ONES", [1, 128], BF16)
    SCR = sb("SCR", [128, 14 * 12], F32)
    CONSTS = sb("CONSTS", [128, 4], F32)
    DEN = sb("DEN", [128, H], F32)
    UTC = STG2[:, 0:256].bitcast(BF16)
    BR1 = STG2[:, 256:768]
    BR2 = STG2[:, 768:1024]
    BR3 = V65[:, 0:1024].bitcast(F32)[:, 0:256]
    B2ROW = PK[:, :]
    BKV = R[0:1, 6144:8192]

    nc.const_aps.register(F32, EPS_LN, CONSTS[:, 0:1])
    nc.const_aps.register(F32, 0.0, CONSTS[:, 1:2])
    nc.const_aps.register(F32, 1.0, CONSTS[:, 2:3])
    nc.const_aps.register(F32, EPS_ATTN, CONSTS[:, 3:4])

    PB = [psa(f"pb{i}", [128, 512], F32) for i in range(6)]
    TRAB = [psa("tra", [128, 512], F32), psa("trb", [128, 512], F32)]
    TRBF = [t[:, :].bitcast(BF16) for t in TRAB]
    TRREG = ["tra", "trb"]
    A_PROJ = [0, 1, 5, 2]
    KVBANKS = [2, 3, 4]
    EOB = [[0, 1], [2, 3]]

    def kv_pair_ap(pr):
        return PB[KVBANKS[pr // 3]][:, (pr % 3) * 130:((pr % 3) + 1) * 130]

    def kv_bank(pr):
        return f"pb{KVBANKS[pr // 3]}"

    ln_id = [0]

    def ln_stats(src_ap, src_regions):
        slot = ln_id[0] % 12
        ln_id[0] += 1
        off = slot * 14
        st = SCR[:, off:off + 12]
        mv = SCR[:, off + 12:off + 14]
        rstd = mv[:, 1:2]
        rg = f"scr{slot}"
        p.stage("dve", [lambda e, a=st, b=src_ap: e.bn_stats(a[:, 0:6], b[:, 0:512]),
                        lambda e, a=st, b=src_ap: e.bn_stats(a[:, 6:12], b[:, 512:1024])],
                reads=list(src_regions), writes=[rg])
        p.stage("dve", [lambda e, a=mv, b=st: e.bn_aggr(a, b)], reads=[rg], writes=[rg])
        p.stage("act", [lambda e, a=rstd, b=mv: e.activation(a, b[:, 1:2], AF.Sqrt,
                                                           bias=EPS_LN, scale=1.0)],
                reads=[rg, "consts"], writes=[rg])
        p.stage("dve", [lambda e, a=rstd: e.reciprocal(a, a)], reads=[rg], writes=[rg])
        return mv[:, 0:1], rstd, rg

    class Streamer:

        def __init__(self, qe, bufs, buf_regions, buf_sems, cast_engs):
            self.qe = qe
            self.bufs = bufs
            self.bregs = buf_regions
            self.bsems = buf_sems
            self.cast_engs = cast_engs
            self.pending = []
            self.ci = 0
            self.bi = 0

        def dma(self, dram_ap, dest_ap, dest_regions, extra=()):
            b = self.bi % len(self.bufs)
            self.bi += 1
            while any(pb == b for (pb, _, _) in self.pending):
                self.flush(1)
            p.stage(self.qe, [lambda e, o=self.bufs[b], i=dram_ap: e.dma_start(o, i)],
                    writes=[self.bregs[b]], extra=list(extra), inc_sem=self.bsems[b])
            self.pending.append((b, dest_ap, dest_regions))

        def flush(self, n=10 ** 9):
            done = 0
            while self.pending and done < n:
                b, dest, dregs = self.pending.pop(0)
                eng = self.cast_engs[self.ci % len(self.cast_engs)]
                self.ci += 1
                if eng == "act":
                    fn = lambda e, o=dest, i=self.bufs[b]: e.activation(o, i, AF.Copy)
                else:
                    fn = lambda e, o=dest, i=self.bufs[b]: e.tensor_copy(o, i)
                p.stage(eng, [fn], reads=[self.bregs[b]], writes=list(dregs))
                done += 1

    wstg_bufs = [WSTG[:, i * 512:(i + 1) * 512] for i in range(2)]
    stg2_bufs = [STG2[:, i * 512:(i + 1) * 512] for i in range(2)]
    gp_st = Streamer("gp", wstg_bufs, ["wstg0", "wstg1"], ["gb0", "gb1"], ["act", "dve"])
    sc_st = Streamer("sync", stg2_bufs, ["stg2b0", "stg2b1"], ["sb0", "sb1"], ["dve", "act"])

    def wq_piece(k, c2):
        return (d_wqkv[k * 128:(k + 1) * 128, c2 * 512:(c2 + 1) * 512],
                R[:, WQOFF + k * 3072 + c2 * 512: WQOFF + k * 3072 + (c2 + 1) * 512],
                [f"wq{k}.{c2}"])

    def w1_piece(k, j):
        return (d_w1[k * 128:(k + 1) * 128, j * 512:(j + 1) * 512],
                W1[:, k * FF + j * 512: k * FF + (j + 1) * 512], [f"w1p{k}.{j}"])

    def w2_piece(f, nn):
        regs = [f"w2p{f}.{nn}"]
        addr = f * 1024 + nn * 512 - WQOFF
        if addr >= 0:
            k = addr // 3072
            c2 = (addr % 3072) // 512
            regs.append(f"wq{k}.{c2}")
        return (d_w2[f * 128:(f + 1) * 128, nn * 512:(nn + 1) * 512],
                R[:, f * 1024 + nn * 512: f * 1024 + (nn + 1) * 512], regs)

    smalls = [
        (IDENT[:, :], d_id[:, :], "ident"),
        (CONSTS[:, :], d_const[:, :], "consts"),
        (BKV, d_bkv[:, :], "bkv"),
        (BQC[:, :], d_bqc[:, :], "bqc"),
        (B1C[:, :], d_b1c[:, :], "b1c"),
    ]
    for (dst, srcd, rg) in smalls:
        p.stage("sync", [lambda e, o=dst, i=srcd: e.dma_start(o, i)],
                writes=[rg], inc_sem="sm")
    sm_all = ("sm", p.cnt["sm"])
    for (_, _, rg) in smalls:
        p.set_writer(rg, sm_all)
    bkv_regions = [f"w2p{fk}.{nn}" for fk in (6, 7) for nn in (0, 1)]
    for rg in bkv_regions:
        p.set_writer(rg, sm_all)

    p.stage("dve", [lambda e: e.memset(ONES[:, :], 1.0)], writes=["ones"])
    p.stage("dve", [lambda e: e.memset(V65[:, :], 1.0)], writes=["v65"])
    p.stage("dve", [lambda e: e.memset(KVB[:, :], 0.0)], writes=["kvb"])

    tx_reg = ["xt0", "xt1", "xt2", "xt3"]
    for sl in range(4):
        p.stage("sync", [lambda e, s=sl: e.dma_start(
            XT[:, s * D:(s + 1) * D], d_x[s * 128:(s + 1) * 128, :])],
                writes=[tx_reg[sl]], inc_sem=f"x{sl}")

    kv_pieces = [(k, c2) for c2 in (2, 3, 4, 5) for k in range(8)]
    q_pieces = [(k, c2) for c2 in (0, 1) for k in range(8)]
    w1_pieces = [(k, j) for j in range(8) for k in range(8)]
    w2e_pieces = [(f, n) for f in range(6) for n in range(2)]
    w2r_pieces = [(f, n) for f in range(6, 32) for n in range(2)]

    nbatches = 4 if trunc is None or trunc[0] != "P" else trunc[1]
    for r in range(nbatches):
        for t in range(4):
            gp_st.dma(*wq_piece(*kv_pieces[r * 8 + t]))
            sc_st.dma(*wq_piece(*kv_pieces[r * 8 + 4 + t]))
        gp_st.flush()
        sc_st.flush()

    proj_i = [0]
    a_subs = NSUB if trunc is None or trunc[0] != "A" else trunc[1]
    bc_grps = NGRP if trunc is None or trunc[0] != "BC" else trunc[1]
    if trunc is not None and trunc[0] in ("A", "P"):
        bc_grps = -1
        if trunc[0] == "P":
            a_subs = 0
    for i in range(a_subs):
        half = i % 2
        slot = i % 4
        xa = XT[:, slot * D:(slot + 1) * D]
        xreg = tx_reg[slot]

        gp_st.flush()
        sc_st.flush()
        if i < 2:
            for t in range(4):
                gp_st.dma(*wq_piece(*q_pieces[i * 8 + t]))
                sc_st.dma(*wq_piece(*q_pieces[i * 8 + 4 + t]))
        elif i < 13:
            for t in range(3):
                idx = (i - 2) * 6 + t * 2
                if idx < 64:
                    gp_st.dma(*w1_piece(*w1_pieces[idx]))
                if idx + 1 < 64:
                    sc_st.dma(*w1_piece(*w1_pieces[idx + 1]))
        if 8 <= i < 14:
            gp_st.dma(*w2_piece(*w2e_pieces[(i - 8) * 2]))
            sc_st.dma(*w2_piece(*w2e_pieces[(i - 8) * 2 + 1]))

        mean, rstd, rg = ln_stats(xa, [xreg])
        p.stage("dve", [lambda e, o=H1B[:, :], a=xa, r=rstd, m=mean:
                        e.tensor_scalar(o, a, m, r, ALU.subtract, ALU.mult)],
                reads=[xreg, rg], writes=["h1b", "attn.0"])

        for hh in range(2):
            fns = []
            for b in range(4):
                c = hh * 4 + b
                fns.append(lambda e, o=TRBF[hh][:, b * 128:(b + 1) * 128],
                           ii=H1B[:, c * 128:(c + 1) * 128]: e.transpose(o, ii, IDENT[:, :]))
            p.stage("pe", fns, reads=["h1b", "ident"], writes=[TRREG[hh]])
            out3d = H1T[:, :].rearrange("p (c t) -> p c t", t=256)[:, hh * 4:hh * 4 + 4,
                                                                  half * 128:(half + 1) * 128]
            in3d = TRBF[hh][:, 0:512].rearrange("p (c t) -> p c t", t=128)
            p.stage("act", [lambda e, o=out3d, ii=in3d: e.copy(o, ii)],
                    reads=[TRREG[hh]], writes=["h1t"])

        for j in range(4):
            bidx = A_PROJ[proj_i[0] % 4]
            proj_i[0] += 1
            bank = PB[bidx]
            c2 = 2 + j
            fns = []
            for k in range(8):
                lhs = H1T[:, k * 256 + half * 128: k * 256 + (half + 1) * 128]
                rhs = R[:, WQOFF + k * 3072 + c2 * 512: WQOFF + k * 3072 + (c2 + 1) * 512]
                fns.append(lambda e, o=bank[:, :], l=lhs, r=rhs, st=(k == 0):
                           e.matmul(o, l, r, start=st, stop=False))
            fns.append(lambda e, o=bank[:, :], r=BKV[0:1, j * 512:(j + 1) * 512]:
                       e.matmul(o, ONES[0:1, 0:128], r, start=False, stop=True))
            p.stage("pe", fns,
                    reads=["h1t", "ones"] + bkv_regions + [f"wq{k}.{c2}" for k in range(8)],
                    writes=[f"pb{bidx}"])
            if j < 2:
                tb = T1[:, (j % 2) * 512:(j % 2) * 512 + 512]
                treg = f"t1.{j % 2}"
                tw = [treg]
                p.stage("dve", [lambda e, o=tb, ii=bank[:, :]:
                                e.tensor_single_scalar(o, ii, 0.0, ALU.min)],
                        reads=[f"pb{bidx}"], writes=tw)
                p.stage("act", [lambda e, o=tb, ii=tb: e.activation(o, ii, AF.Exp)],
                        reads=[treg], writes=tw)
                p.stage("dve", [lambda e, o=PK[:, j * 512:(j + 1) * 512], ii=bank[:, :], t=tb:
                                e.scalar_tensor_tensor(o, ii, 0.0, t, ALU.max, ALU.add)],
                        reads=[f"pb{bidx}", treg], writes=["pk"])
            else:
                hb = (j - 2) * 8
                out_v = V65[:, :].rearrange("p (h m) -> p h m", m=65)[:, hb:hb + 8, 0:64]
                in_v = bank[:, :].rearrange("p (h m) -> p h m", m=64)
                p.stage("dve", [lambda e, o=out_v, ii=in_v: e.tensor_copy(o, ii)],
                        reads=[f"pb{bidx}"], writes=["v65"])

        for half8 in range(2):
            fns = [lambda e, o=kv_pair_ap(pr), l=PK[:, pr * 128:(pr + 1) * 128],
                   r=V65[:, pr * 130:(pr + 1) * 130]:
                   e.matmul(o, l, r, start=True, stop=True)
                   for pr in range(half8 * 4, half8 * 4 + 4)]
            p.stage("pe", fns, reads=["pk", "v65"],
                    writes=sorted(set(kv_bank(pr) for pr in range(half8 * 4, half8 * 4 + 4))))
        for bi, (b0, wid) in enumerate(((0, 390), (390, 390), (780, 260))):
            bank = PB[KVBANKS[bi]]
            if i == 0:
                p.stage("dve", [lambda e, o=KVSB[:, b0:b0 + wid], ii=bank[:, 0:wid]:
                                e.tensor_copy(o, ii)],
                        reads=[f"pb{KVBANKS[bi]}"], writes=[f"kvsb{bi}"])
            else:
                p.stage("dve", [lambda e, o=KVSB[:, b0:b0 + wid], ii=bank[:, 0:wid]:
                                e.tensor_tensor(o, o, ii, ALU.add)],
                        reads=[f"pb{KVBANKS[bi]}", f"kvsb{bi}"], writes=[f"kvsb{bi}"])

        nxt = i + 4
        if nxt < 2 * NSUB:
            a = nxt if nxt < NSUB else nxt - NSUB
            dst = XT[:, (nxt % 4) * D:((nxt % 4) + 1) * D]
            p.stage("sync", [lambda e, o=dst, aa=a:
                             e.dma_start(o, d_x[aa * 128:(aa + 1) * 128, :])],
                    writes=[tx_reg[nxt % 4]], inc_sem=f"x{nxt % 4}")

        if i % 2 == 1:
            tile = i // 2
            gp_st.flush()
            sc_st.flush()
            for c in range(8):
                bidx = A_PROJ[proj_i[0] % 4]
                proj_i[0] += 1
                bank = PB[bidx]
                fns = []
                for k in range(8):
                    lhs = R[:, WQOFF + k * 3072 + c * 128: WQOFF + k * 3072 + (c + 1) * 128]
                    rhs = H1T[:, k * 256:(k + 1) * 256]
                    fns.append(lambda e, o=bank[:, 0:256], l=lhs, r=rhs,
                               st=(k == 0), sp=(k == 7): e.matmul(o, l, r, start=st, stop=sp))
                p.stage("pe", fns,
                        reads=["h1t"] + [f"wq{k}.{(c * 128) // 512}" for k in range(8)],
                        writes=[f"pb{bidx}"])
                ta = T1[:, (c % 2) * 512:(c % 2) * 512 + 256]
                tc = T1[:, (c % 2) * 512 + 256:(c % 2) * 512 + 512]
                tb2 = ATTN[:, (c % 2) * 512:(c % 2) * 512 + 256]
                treg, areg = f"t1.{c % 2}", f"attn.{c % 2}"
                p.stage("dve", [lambda e, o=ta, ii=bank[:, 0:256], bq=BQC[:, c:c + 1]:
                                e.tensor_scalar(o, ii, bq, 0.0, ALU.add, ALU.min),
                                lambda e, o=tc, ii=bank[:, 0:256], bq=BQC[:, c:c + 1]:
                                e.tensor_scalar(o, ii, bq, 0.0, ALU.add, ALU.max)],
                        reads=[f"pb{bidx}", "bqc"], writes=[treg])
                p.stage("act", [lambda e, o=tb2, ii=ta: e.activation(o, ii, AF.Exp)],
                        reads=[treg],
                        writes=[areg] + (["h1b"] if c % 2 == 0 else []))
                dst = PQT[:, c * S + tile * 256: c * S + tile * 256 + 256]
                p.stage("dve", [lambda e, o=dst, a=tc, b=tb2: e.tensor_add(o, a, b)],
                        reads=[treg, areg], writes=["pqt"])

    for pr in range(8 if a_subs == NSUB else 0):
        p.stage("dve", [lambda e, o=KVB[0:64, pr * 130: pr * 130 + 65],
                        ii=KVSB[0:64, pr * 130: pr * 130 + 65]: e.tensor_copy(o, ii)],
                reads=[f"kvsb{pr // 3}"], writes=["kvb"])
        p.stage("dve", [lambda e, o=KVB[64:128, pr * 130 + 65: pr * 130 + 130],
                        ii=KVSB[64:128, pr * 130 + 65: pr * 130 + 130]: e.tensor_copy(o, ii)],
                reads=[f"kvsb{pr // 3}"], writes=["kvb"])

    grep_regions = [f"kvsb{pr2}" for pr2 in range(3)]
    if bc_grps >= 0:
        p.stage("sync", [lambda e: e.dma_start(GREP, d_grep[:, :])],
                writes=grep_regions, inc_sem="sm2")

    gp_st.flush()
    sc_st.flush()

    pkv_bufs = [PK[:, :].bitcast(F32), V65[:, 0:1024].bitcast(F32)]
    w2r_gp = Streamer("gp", wstg_bufs, ["wstg0", "wstg1"], ["gb0", "gb1"], ["dve", "act"])
    w2r_gp.bi = gp_st.bi
    w2r_sc = Streamer("sync", pkv_bufs, ["pk", "v65"], ["kb0", "kb1"], ["act", "dve"])
    w2r_queue = list(w2r_pieces)
    if bc_grps >= 1:
        for _ in range(2):
            w2r_gp.dma(*w2_piece(*w2r_queue.pop(0)))
            w2r_sc.dma(*w2_piece(*w2r_queue.pop(0)))

    def bc_front(g):
        infos = []
        for s2 in range(2):
            gi = NSUB + g * 2 + s2
            tok0 = g * 256 + s2 * 128
            xa = XT[:, (gi % 4) * D:((gi % 4) + 1) * D]
            xreg = tx_reg[gi % 4]

            s1e = (2, 3) if g % 2 == 0 else (4, 5)
            at_banks = [(TRAB[0], "tra", (0, 1, 2)), (TRAB[1], "trb", (3, 4, 5)),
                        (PB[s1e[0]], f"pb{s1e[0]}", (6,)),
                        (PB[s1e[1]], f"pb{s1e[1]}", (7,))]
            for (bank, breg, prs) in at_banks:
                fns = [lambda e, o=bank[:, qq * 130:(qq + 1) * 130],
                       l=PQT[:, pr * S + tok0: pr * S + tok0 + 128],
                       r=KVB[:, pr * 130:(pr + 1) * 130]:
                       e.matmul(o, l, r, start=True, stop=True)
                       for qq, pr in enumerate(prs)]
                p.stage("pe", fns, reads=["pqt", "kvb"], writes=[breg])
            fns = []
            for (bank, breg, prs) in at_banks:
                nds = 2 * len(prs)
                den_s = DEN[:, 2 * prs[0]: 2 * prs[0] + nds]
                in_d = bank[:, 0:len(prs) * 130].rearrange(
                    "p (x m) -> p x m", m=65)[:, :, 64:65].squeeze(2)
                fns.append(lambda e, o=den_s, ii=in_d: e.tensor_copy(o, ii))
            p.stage("dve", fns,
                    reads=["tra", "trb", f"pb{s1e[0]}", f"pb{s1e[1]}"],
                    writes=["den", "zr"])
            p.stage("dve", [lambda e: e.tensor_single_scalar(DEN[:, :], DEN[:, :],
                                                             EPS_ATTN, ALU.add)],
                    reads=["den"], writes=["den"])
            p.stage("dve", [lambda e: e.reciprocal(DEN[:, :], DEN[:, :])],
                    reads=["den"], writes=["den", "zr"])
            fns = []
            for (bank, breg, prs) in at_banks[:2]:
                for qq, pr in enumerate(prs):
                    for hh2 in range(2):
                        hd = 2 * pr + hh2
                        srcp = bank[:, qq * 130 + hh2 * 65: qq * 130 + hh2 * 65 + 64]
                        fns.append(lambda e, o=ATTN[:, hd * 64:(hd + 1) * 64],
                                   ii=srcp, z=DEN[:, hd:hd + 1]:
                                   e.activation(o, ii, AF.Copy, bias=0.0, scale=z))
            p.stage("act", fns, reads=["tra", "trb", "zr", "consts"],
                    writes=["attn.0", "attn.1", "h1b"])
            fns = []
            for (bank, breg, prs) in at_banks[2:]:
                for qq, pr in enumerate(prs):
                    for hh2 in range(2):
                        hd = 2 * pr + hh2
                        srcp = bank[:, qq * 130 + hh2 * 65: qq * 130 + hh2 * 65 + 64]
                        fns.append(lambda e, o=ATTN[:, hd * 64:(hd + 1) * 64],
                                   ii=srcp, z=DEN[:, hd:hd + 1]:
                                   e.tensor_scalar(o, ii, z, 0.0, ALU.mult, ALU.add))
            p.stage("dve", fns,
                    reads=[f"pb{s1e[0]}", f"pb{s1e[1]}", "zr"],
                    writes=["attn.0", "attn.1", "h1b"])

            mean, rstd, rg = ln_stats(ATTN[:, :], ["attn.0", "attn.1"])
            p.stage("dve", [lambda e, o=ATTN[:, :], a=ATTN[:, :], r=rstd, m=mean:
                            e.tensor_scalar(o, a, m, r, ALU.subtract, ALU.mult)],
                    reads=["attn.0", "attn.1", rg],
                    writes=["attn.0", "attn.1", "h1b"])
            p.stage("dve", [lambda e, o=xa, a=xa, b=ATTN[:, :]: e.tensor_add(o, a, b)],
                    reads=["attn.0", "attn.1", xreg], writes=[xreg])

            mean2, rstd2, rg2 = ln_stats(xa, [xreg])
            p.stage("dve", [lambda e, o=H1B[:, :], a=xa, r=rstd2, m=mean2:
                            e.tensor_scalar(o, a, m, r, ALU.subtract, ALU.mult)],
                    reads=[xreg, rg2], writes=["h1b", "attn.0"])

            for hh in range(2):
                fns = []
                for b in range(4):
                    c = hh * 4 + b
                    fns.append(lambda e,
                               o=TRBF[hh][:, b * 128:(b + 1) * 128],
                               ii=H1B[:, c * 128:(c + 1) * 128]:
                               e.transpose(o, ii, IDENT[:, :]))
                p.stage("pe", fns, reads=["h1b", "ident"], writes=[TRREG[hh]])
                out3d = H1T[:, :].rearrange("p (c t) -> p c t", t=256)[:, hh * 4:hh * 4 + 4,
                                                                      s2 * 128:(s2 + 1) * 128]
                in3d = TRBF[hh][:, 0:512].rearrange("p (c t) -> p c t", t=128)
                p.stage("act", [lambda e, o=out3d, ii=in3d: e.copy(o, ii)],
                        reads=[TRREG[hh]], writes=["h1t"])
            infos.append(dict(xa=xa, xreg=xreg, gi=gi, tok0=tok0,
                              rstd2=rstd2, mean2=mean2, rg2=rg2))
        return infos

    def bc_ffn(g, epi_chunks=(), y_infos=None):
        s1e = (2, 3) if g % 2 == 0 else (4, 5)
        eob = [[0, 1], list(s1e)]

        def emit_w1(f):
            uh = f % 2
            ut = TRAB[uh][:, 0:256]
            fns = []
            for k in range(8):
                lhs = W1[:, k * FF + f * 128: k * FF + f * 128 + 128]
                rhs = H1T[:, k * 256: k * 256 + 256]
                fns.append(lambda e, o=ut, l=lhs, r=rhs, st=(k == 0), sp=(k == 7):
                           e.matmul(o, l, r, start=st, stop=sp))
            p.stage("pe", fns,
                    reads=["h1t"] + [f"w1p{k}.{(f * 128) // 512}" for k in range(8)],
                    writes=[TRREG[uh]])
            p.stage("act", [lambda e, o=UTC[:, uh * 256: uh * 256 + 256],
                            ii=ut, b=B1C[:, f:f + 1]:
                            e.activation(o, ii, AF.Relu, bias=b, scale=1.0)],
                    reads=[TRREG[uh], "b1c", "consts"], writes=[f"utc{uh}", "stg2b0"])

        eoregs = [f"pb{b}" for row in eob for b in row]

        def emit_w2(f):
            uh = f % 2
            fns = []
            for s2 in range(2):
                for nn2 in range(2):
                    eo = PB[eob[s2][nn2]]
                    lhs = UTC[:, uh * 256 + s2 * 128: uh * 256 + s2 * 128 + 128]
                    rhs = R[:, f * 1024 + nn2 * 512: f * 1024 + (nn2 + 1) * 512]
                    fns.append(lambda e, o=eo[:, :], l=lhs, r=rhs, st=(f == 0):
                               e.matmul(o, l, r, start=st, stop=False))
            for nn2 in (0, 1):
                wrec = p.reg.get(f"w2p{f}.{nn2}", {}).get("w")
                assert wrec is not None and wrec[0] in ("act", "dve")
            p.stage("pe", fns,
                    reads=[f"utc{uh}", f"w2p{f}.0", f"w2p{f}.1"],
                    writes=list(eoregs))

        for f in range(32):
            if g == 0 and w2r_queue:
                w2r_gp.dma(*w2_piece(*w2r_queue.pop(0)))
                w2r_sc.dma(*w2_piece(*w2r_queue.pop(0)))
            w2r_gp.flush(1)
            w2r_sc.flush(1)
            emit_w1(f)
            if f >= 1:
                emit_w2(f - 1)
            if f >= 6 and f % 6 == 0 and (f - 6) // 6 < len(epi_chunks):
                epi_chunks[(f - 6) // 6]()
        emit_w2(31)
        if g == 0:
            p.stage("sync", [lambda e: e.dma_start(BR1, d_brep[:, 0:512])],
                    writes=["stg2b0", "stg2b1", "brep"], inc_sem="sm")
            p.stage("sync", [lambda e: e.dma_start(BR2, d_brep[:, 512:768])],
                    writes=["stg2b1", "brep"], inc_sem="sm")
            p.stage("sync", [lambda e: e.dma_start(BR3, d_brep[:, 768:1024])],
                    writes=["pk", "v65", "brep"], inc_sem="sm")
            p.stage("sync", [lambda e: e.dma_start(B2ROW[0:1, :], d_b2[:, :])],
                    writes=["pk", "b2row"], inc_sem="sm")
        fns = []
        for s2 in range(2):
            for nn2 in range(2):
                eo = PB[eob[s2][nn2]]
                fns.append(lambda e, o=eo[:, :], r=B2ROW[0:1, nn2 * 512:(nn2 + 1) * 512]:
                           e.matmul(o, ONES[0:1, 0:128], r, start=False, stop=True))
        p.stage("pe", fns, reads=["ones", "b2row"], writes=list(eoregs))
        if y_infos is not None:
            for s2 in range(2):
                si = y_infos[s2]
                ybuf = WSTG if s2 == 0 else T1
                yregs = ["wstg0", "wstg1"] if s2 == 0 else ["t1.0", "t1.1"]
                p.stage("dve", [lambda e, o=ybuf[:, :], a=si["xa"], r=si["rstd2"],
                                m=si["mean2"]:
                                e.tensor_scalar(o, a, m, r, ALU.subtract, ALU.mult)],
                        reads=[si["xreg"], si["rg2"]], writes=list(yregs))

    def bc_epi_y(g, sub_info):
        s1e = (2, 3) if g % 2 == 0 else (4, 5)
        for s2 in range(2):
            ybuf = WSTG if s2 == 0 else T1
            yregs = ["wstg0", "wstg1"] if s2 == 0 else ["t1.0", "t1.1"]
            banks = (0, 1) if s2 == 0 else s1e
            for nn2 in range(2):
                eo = PB[banks[nn2]]
                p.stage("dve", [lambda e, o=ybuf[:, nn2 * 512:(nn2 + 1) * 512],
                                a=ybuf[:, nn2 * 512:(nn2 + 1) * 512], b=eo[:, :]:
                                e.tensor_tensor(o, a, b, ALU.add)],
                        reads=[f"pb{banks[nn2]}", yregs[nn2]], writes=[yregs[nn2]])

    def bc_epi_rest_chunks(g, sub_info):
        chunks = []
        for s2 in range(2):
            si = sub_info[s2]
            xa, xreg, gi, tok0 = si["xa"], si["xreg"], si["gi"], si["tok0"]
            ybuf = WSTG if s2 == 0 else T1
            yregs = ["wstg0", "wstg1"] if s2 == 0 else ["t1.0", "t1.1"]

            def phase0(xa=xa, xreg=xreg, ybuf=ybuf, yregs=yregs):
                mean, rstd, rg = ln_stats(ybuf[:, :], yregs)
                p.stage("dve", [lambda e, o=ybuf[:, :], a=ybuf[:, :], r=rstd, m=mean:
                                e.tensor_scalar(o, a, m, r, ALU.subtract, ALU.mult)],
                        reads=yregs + [rg], writes=list(yregs))
                p.stage("dve", [lambda e, o=ybuf[:, :], a=ybuf[:, :]:
                                e.tensor_mul(o, a, GREP)],
                        reads=yregs + grep_regions, writes=list(yregs))
                p.stage("dve", [lambda e, o=ybuf[:, 0:512], a=ybuf[:, 0:512]:
                                e.tensor_add(o, a, BR1),
                                lambda e, o=ybuf[:, 512:768], a=ybuf[:, 512:768]:
                                e.tensor_add(o, a, BR2),
                                lambda e, o=ybuf[:, 768:1024], a=ybuf[:, 768:1024]:
                                e.tensor_add(o, a, BR3)],
                        reads=yregs + ["brep"], writes=list(yregs))

            def phase1(xa=xa, xreg=xreg, gi=gi, tok0=tok0, ybuf=ybuf, yregs=yregs):
                mean3, rstd3, rg3 = ln_stats(ybuf[:, :], yregs)
                p.stage("dve", [lambda e, o=ybuf[:, :], a=ybuf[:, :], r=rstd3, m=mean3:
                                e.tensor_scalar(o, a, m, r, ALU.subtract, ALU.mult)],
                        reads=yregs + [rg3], writes=list(yregs))
                p.stage("dve", [lambda e, o=xa, a=xa, b=ybuf[:, :]:
                                e.tensor_add(o, a, b)],
                        reads=yregs + [xreg], writes=[xreg])
                p.stage("sync", [lambda e, o=d_out[tok0:tok0 + 128, :], ii=xa:
                                 e.dma_start(o, ii)],
                        reads=[xreg], inc_sem=f"o{gi % 4}")
                nxt = gi + 4
                if nxt < 2 * NSUB:
                    a = nxt - NSUB
                    dst = XT[:, (nxt % 4) * D:((nxt % 4) + 1) * D]
                    p.stage("sync", [lambda e, o=dst, aa=a:
                                     e.dma_start(o, d_x[aa * 128:(aa + 1) * 128, :])],
                            writes=[tx_reg[nxt % 4]], inc_sem=f"x{nxt % 4}")

            chunks.append(phase0)
            chunks.append(phase1)
        return chunks

    infos = bc_front(0) if bc_grps >= 1 else None
    prev_infos = None
    for g in range(max(bc_grps, 0)):
        chunks = bc_epi_rest_chunks(g - 1, prev_infos) if prev_infos is not None else ()
        bc_ffn(g, chunks, infos)
        bc_epi_y(g, infos)
        nxt_infos = bc_front(g + 1) if g + 1 < bc_grps else None
        prev_infos = infos
        infos = nxt_infos
    if prev_infos is not None:
        for ch in bc_epi_rest_chunks(bc_grps - 1, prev_infos):
            ch()

    w2r_gp.flush()
    w2r_sc.flush()
    for osl in range(4):
        if p.cnt[f"o{osl}"]:
            p.raw_wait("sync", (f"o{osl}", p.cnt[f"o{osl}"]))
    if trunc is not None:
        for sname in ("pe", "act", "dve"):
            if p.cnt[sname]:
                p.raw_wait("sync", (sname, p.cnt[sname]))

    with stack:
        sems = {}
        for sname in sorted(p.cnt.keys()):
            sems[sname] = stack.enter_context(nc.semaphore(sname))
        block = stack.enter_context(nc.Block())

        def emitter(key):
            def run(eng):
                for (waits, fn, inc) in p.items[key]:
                    for (s, v) in waits:
                        eng.wait_ge(sems[s], v)
                    if fn is None:
                        continue
                    ins = fn(eng)
                    if inc is not None:
                        ins.then_inc(sems[inc[0]], inc[1])
            return run

        block.tensor(emitter("pe"))
        block.scalar(emitter("act"))
        block.vector(emitter("dve"))
        block.sync(emitter("sync"))
        block.gpsimd(emitter("gp"))

    return nc


_NC_CACHE = []


def kernel(L0_attn_x, L0_attn_W_qkv, L0_attn_b_qkv, L0_attn_W_proj, L0_attn_b_proj,
           L0_ffn_router_w, L0_ffn_router_b, L0_ffn_w1, L0_ffn_b1, L0_ffn_w2,
           L0_ffn_b2, L0_ffn_gamma, L0_ffn_beta):
    x = np.ascontiguousarray(np.asarray(L0_attn_x, dtype=np.float32))
    wqkv = np.ascontiguousarray(np.asarray(L0_attn_W_qkv, dtype=np.float32))
    bqkv = np.asarray(L0_attn_b_qkv, dtype=np.float32)
    w1 = np.ascontiguousarray(np.asarray(L0_ffn_w1, dtype=np.float32)[0])
    b1 = np.asarray(L0_ffn_b1, dtype=np.float32)[0]
    w2 = np.ascontiguousarray(np.asarray(L0_ffn_w2, dtype=np.float32)[0])
    b2 = np.asarray(L0_ffn_b2, dtype=np.float32)[0]
    gamma = np.asarray(L0_ffn_gamma, dtype=np.float32)
    beta = np.asarray(L0_ffn_beta, dtype=np.float32)

    bkv = np.ascontiguousarray(bqkv[D:3 * D][None, :].astype(BFnp))
    bqc = np.ascontiguousarray(bqkv[0:D].reshape(8, 128).T)
    b1c = np.ascontiguousarray(b1.reshape(32, 128).T)
    b2row = np.ascontiguousarray(b2[None, :].astype(BFnp))
    grep = np.ascontiguousarray(np.broadcast_to(gamma[None, :], (128, D)))
    brep = np.ascontiguousarray(np.broadcast_to(beta[None, :], (128, D)))
    ident = np.eye(128, dtype=BFnp)
    consts = np.ascontiguousarray(
        np.tile(np.array([[EPS_LN, 0.0, 1.0, EPS_ATTN]], dtype=np.float32), (128, 1)))

    if not _NC_CACHE:
        _NC_CACHE.append(build_program())
    nc = _NC_CACHE[0]
    in_maps = []
    for c in range(8):
        in_maps.append({
            "x": np.ascontiguousarray(x[c]),
            "wqkv": wqkv, "bkv": bkv, "bqc": bqc,
            "w1": w1, "b1c": b1c, "w2": w2, "b2row": b2row,
            "grep": grep, "brep": brep, "ident": ident, "consts": consts,
        })
    res = run_bass_kernel_spmd(nc, in_maps, core_ids=list(range(8)))
    out = np.stack([np.asarray(res.results[c]["out"]) for c in range(8)], axis=0)
    return out.astype(np.float32)
